# version 04
# baseline (speedup 1.0000x reference)
import numpy as np
import ml_dtypes
from contextlib import ExitStack
import concourse.bass as bass
import concourse.mybir as mybir
from concourse.bass_utils import run_bass_kernel_spmd

F32 = mybir.dt.float32
BF16 = mybir.dt.bfloat16
I8 = mybir.dt.int8
I32 = mybir.dt.int32
AF = mybir.ActivationFunctionType
ALU = mybir.AluOpType
AX = mybir.AxisListType.X

T = 1024
D = 2048
KT = D // 128
C2 = 2
SCALE = float(1.0 / np.sqrt(128.0))
EPS = 1e-5
RG_PAIRS = [[0, 1], [2, 3], [4, 5], [6, 7]]

DEBUG_TAPS = False
DETECT_RACES = False

GROUPS = [[0, 1, 2, 3, 4, 5]]


def _np(a):
    return np.asarray(a)

def mslab(w):
    K, M = w.shape
    return np.ascontiguousarray(
        w.reshape(K // 128, 128, M // 128, 128).transpose(2, 1, 0, 3).reshape(M // 128, 128, K))

def pack_feat(v):
    v = _np(v).astype(np.float32)
    return np.ascontiguousarray(v.reshape(-1, 128).T)

_CONST_SPEC = [("l0_bq", 16), ("l0_bk", 4), ("l0_bo", 16), ("l0_sg", 64),
               ("l0_su", 64), ("l0_sd", 16), ("l1_bq", 16), ("l1_bk", 4),
               ("gbias", 8), ("scratch", 16), ("eps", 1), ("l0_bvb", 512), ("l1_bvb", 512)]
CONST_OFF = {}
_c = 0
for _n, _w in _CONST_SPEC:
    CONST_OFF[_n] = _c
    _c += _w
NCONST = _c

CHUNKS = [(0, 3), (1, 2)]
KCB = [0, 1536, 512, 1024]
LIVE_T = {0: [0, 1, 2, 3, 8, 9, 10, 11], 1: list(range(16))}
MASK_T = {0: {0, 1, 2, 3, 8, 9, 10, 11}, 1: {4, 5, 6, 7, 12, 13, 14, 15}}
SOFF = 2048


def prep_inputs(inputs):
    p = {}
    x = _np(inputs["L0_attn_x"]).reshape(4, 4, 512, 2048)
    p["x_shards"] = [
        np.ascontiguousarray(
            np.concatenate([x[c // 2, ch] for ch in CHUNKS[c % 2]], axis=0).T)
        for c in range(8)]

    p["l0_wq"] = mslab(_np(inputs["L0_attn_Wq"]).astype(np.float32))
    p["l0_wk"] = mslab(_np(inputs["L0_attn_Wk"]).astype(np.float32))
    p["l0_wvk"] = np.ascontiguousarray(
        _np(inputs["L0_attn_Wv"]).astype(np.float32).reshape(16, 128, 512))
    p["l0_wo"] = mslab(_np(inputs["L0_attn_Wo"]).astype(np.float32))
    p["l0_wg"] = mslab(_np(inputs["L0_ffn_w_gate"]))
    p["l0_wu"] = mslab(_np(inputs["L0_ffn_w_up"]))
    p["l0_wd"] = mslab(_np(inputs["L0_ffn_w_down"]))

    p["l1_wq"] = mslab(_np(inputs["L1_attn_Wq"]))
    p["l1_wk"] = mslab(_np(inputs["L1_attn_Wk"]))
    p["l1_wvk"] = np.ascontiguousarray(_np(inputs["L1_attn_Wv"]).reshape(16, 128, 512))
    p["l1_wo"] = mslab(_np(inputs["L1_attn_Wo"]))

    gw = _np(inputs["L1_ffn_gate_w"]).astype(np.float32)
    gw_pad = np.zeros((2048, 128), np.float32)
    gw_pad[:, :8] = gw.T
    p["l1_gw"] = mslab(gw_pad)
    p["l1_sgw"] = mslab(np.ascontiguousarray(_np(inputs["L1_ffn_shared_gw"]).astype(np.float32).T))
    p["l1_sdw"] = mslab(np.ascontiguousarray(_np(inputs["L1_ffn_shared_dw"]).astype(np.float32).T))
    up = _np(inputs["L1_ffn_up_w"]).astype(np.float32)
    dn = _np(inputs["L1_ffn_down_w"]).astype(np.float32)
    p["l1_up"] = np.stack([mslab(np.ascontiguousarray(up[e].T)) for e in range(8)])
    p["l1_dn"] = np.stack([mslab(np.ascontiguousarray(dn[e].T)) for e in range(8)])

    cn = np.zeros((128, NCONST), np.float32)
    def put(name, v):
        o = CONST_OFF[name]
        v = np.asarray(v, np.float32)
        cn[:, o:o + v.shape[1]] = v
    put("l0_bq", pack_feat(inputs["L0_attn_bq"]))
    put("l0_bk", pack_feat(inputs["L0_attn_bk"]))
    put("l0_bo", pack_feat(inputs["L0_attn_bo"]))
    put("l0_sg", pack_feat(inputs["L0_ffn_s_gate"]))
    put("l0_su", pack_feat(inputs["L0_ffn_s_up"]))
    put("l0_sd", pack_feat(inputs["L0_ffn_s_down"]))
    put("l1_bq", pack_feat(_np(inputs["L1_attn_bq"]).astype(np.float32)))
    put("l1_bk", pack_feat(_np(inputs["L1_attn_bk"]).astype(np.float32)))
    put("gbias", np.broadcast_to(_np(inputs["L1_ffn_gate_bias"]).astype(np.float32), (128, 8)))
    cn[:, CONST_OFF["eps"]] = EPS
    put("l0_bvb", np.broadcast_to(_np(inputs["L0_attn_bv"]).astype(np.float32), (128, 512)))
    put("l1_bvb", np.broadcast_to(_np(inputs["L1_attn_bv"]).astype(np.float32), (128, 512)))
    p["consts"] = cn
    p["pos_off"] = [
        np.broadcast_to(np.array([[512.0 * CHUNKS[c % 2][0], 512.0 * CHUNKS[c % 2][1]]],
                                 np.float32), (128, 2)).copy()
        for c in range(8)]
    return p

SUB_WEIGHTS = {
    0: ["l0_wg", "l0_wu", "l0_wd"],
    1: ["l0_wq", "l0_wk", "l0_wvk", "l0_wo"],
    2: ["l0_wg", "l0_wu", "l0_wd"],
    3: ["l1_gw", "l1_sgw", "l1_sdw", "l1_up", "l1_dn"],
    4: ["l1_wq", "l1_wk", "l1_wvk", "l1_wo"],
    5: ["l1_gw", "l1_sgw", "l1_sdw", "l1_up", "l1_dn"],
}
W_DTYPE = {"l0_wq": F32, "l0_wk": F32, "l0_wvk": F32, "l0_wo": F32,
           "l0_wg": I8, "l0_wu": I8, "l0_wd": I8,
           "l1_wq": BF16, "l1_wk": BF16, "l1_wvk": BF16, "l1_wo": BF16,
           "l1_gw": F32, "l1_sgw": F32, "l1_sdw": F32, "l1_up": F32, "l1_dn": F32}
W_SHAPE = {"l0_wq": [16, 128, 2048], "l0_wk": [4, 128, 2048], "l0_wvk": [16, 128, 512],
           "l0_wo": [16, 128, 2048], "l0_wg": [64, 128, 2048], "l0_wu": [64, 128, 2048],
           "l0_wd": [16, 128, 8192],
           "l1_wq": [16, 128, 2048], "l1_wk": [4, 128, 2048], "l1_wvk": [16, 128, 512],
           "l1_wo": [16, 128, 2048],
           "l1_gw": [1, 128, 2048], "l1_sgw": [44, 128, 2048], "l1_sdw": [16, 128, 2816],
           "l1_up": [8, 22, 128, 2048], "l1_dn": [8, 16, 128, 1408]}


COMPUTE_STREAMS = {"pe": "tensor", "dve": "vector", "act": "scalar", "gpc": "gpsimd"}
DMA_CHANS = ["io", "kv", "w0", "w1", "w2", "w3", "w4", "w5"]

class Prog:
    def __init__(self):
        self.echo_fn = None
        self.ops = []
        self.sops = {s: [] for s in list(COMPUTE_STREAMS) + DMA_CHANS}
        self.bufs = {}

    def _buf(self, k):
        return self.bufs.setdefault(k, dict(writer=None, readers=[], pending=set()))

    def op(self, stream, fn, reads=(), writes=(), engine=None):
        if engine is None:
            engine = COMPUTE_STREAMS[stream]
        reads = list(dict.fromkeys(reads))
        writes = list(dict.fromkeys(writes))
        deps = set()
        for k in reads:
            b = self._buf(k)
            deps |= b["pending"]
            b["pending"] = set()
            if b["writer"] is not None:
                deps.add(b["writer"])
        for k in writes:
            b = self._buf(k)
            deps |= b["pending"]
            b["pending"] = set()
            if b["writer"] is not None:
                deps.add(b["writer"])
            deps.update(b["readers"])
        evt = (stream, len(self.sops[stream]))
        dmap = {}
        for (s, q) in deps:
            if s == stream and s == "pe":
                continue
            dmap[s] = max(dmap.get(s, -1), q)
        o = dict(stream=stream, fn=fn, deps=dmap, marked=False, engine=engine)
        self.ops.append(o)
        self.sops[stream].append(o)
        for k in reads:
            self._buf(k)["readers"].append(evt)
        for k in writes:
            b = self._buf(k)
            b["writer"] = evt
            b["readers"] = []
        return evt

    def dma(self, chan, engine, fn, reads=(), writes=(), echo=False):
        evt = self.op(chan, fn, reads=reads, writes=writes, engine=engine)
        if echo and self.echo_fn is not None:
            eevt = self.op(chan, self.echo_fn, engine=engine)
            for k in writes:
                self.bufs[k]["writer"] = eevt
        return evt

    def retag(self, new_keys, old_keys):
        pend = set()
        for k in old_keys:
            if k in self.bufs:
                b = self.bufs.pop(k)
                if b["writer"] is not None:
                    pend.add(b["writer"])
                pend.update(b["readers"])
                pend |= b["pending"]
        for k in new_keys:
            b = self._buf(k)
            b["pending"] |= pend

    def finalize_and_emit(self, nc, es):
        step = lambda s: 16 if s in DMA_CHANS else 1
        for o in self.ops:
            for s, q in o["deps"].items():
                self.sops[s][q]["marked"] = True
        for s in DMA_CHANS:
            for o in self.sops[s]:
                o["marked"] = True
        finals = {}
        for s, ops in self.sops.items():
            c = 0
            for o in ops:
                if o["marked"]:
                    c += 1
                o["count"] = c * step(s)
            finals[s] = c * step(s)
        sems = {s: es.enter_context(nc.semaphore(f"sem_{s}")) for s in self.sops}
        engines = ["tensor", "vector", "scalar", "gpsimd", "sync"]
        by_engine = {e: [] for e in engines}
        for o in self.ops:
            by_engine[o["engine"]].append(o)

        def runner(eng_obj, engine_name):
            waited = {}
            for o in by_engine[engine_name]:
                if o["stream"] in DMA_CHANS and o["count"] > 16:
                    v = o["count"] - 16
                    if v > waited.get(o["stream"], 0):
                        eng_obj.wait_ge(sems[o["stream"]], v)
                        waited[o["stream"]] = v
                for s in sorted(o["deps"]):
                    q = o["deps"][s]
                    dep = self.sops[s][q]
                    assert dep["marked"]
                    v = dep["count"]
                    if v > waited.get(s, 0):
                        eng_obj.wait_ge(sems[s], v)
                        waited[s] = v
                ins = o["fn"](eng_obj)
                if o["marked"]:
                    ins.then_inc(sems[o["stream"]], step(o["stream"]))
            for s in sorted(self.sops):
                if finals[s] > waited.get(s, 0):
                    eng_obj.wait_ge(sems[s], finals[s])

        with nc.Block() as blk:
            blk.sync(lambda e: runner(e, "sync"))
            blk.gpsimd(lambda e: runner(e, "gpsimd"))
            blk.vector(lambda e: runner(e, "vector"))
            blk.scalar(lambda e: runner(e, "scalar"))
            blk.tensor(lambda e: runner(e, "tensor"))

def mm_accum(P, bank_ap, bank_key, parts):
    n = len(parts)
    for i, (l, r, rk) in enumerate(parts):
        P.op("pe",
             (lambda pe, l=l, r=r, st=(i == 0), sp=(i == n - 1):
              pe.matmul(bank_ap, l, r, start=st, stop=sp)),
             reads=list(rk), writes=[bank_key])


class Ctx:
    pass

def atk(lo, hi):
    return [("at", i) for i in range(lo // 512, (hi + 511) // 512)]

def build_program(subs):
    nc = bass.Bass("TRN2", target_bir_lowering=False,
                   detect_race_conditions=DETECT_RACES)
    es = ExitStack()
    X = Ctx()
    X.in_names = []
    X.nc = nc
    X.es = es

    def din(name, shape, dt):
        X.in_names.append(name)
        return nc.dram_tensor(name, list(shape), dt, kind="ExternalInput").ap()

    x_in = din("x_in", [D, T], F32)
    consts_in = din("consts", [128, NCONST], F32)
    pos_in = din("pos_off", [128, 2], F32)
    w_in = {}
    for w in sorted({w for s in subs for w in SUB_WEIGHTS.get(s, [])}):
        w_in[w] = din(w, W_SHAPE[w], W_DTYPE[w])
    x_out = nc.dram_tensor("x_out", [D, T], F32, kind="ExternalOutput").ap()

    echo_dram = nc.dram_tensor("echo_dram", [1, 64], F32, kind="Internal").ap()

    kvc = kva = None
    if any(s in (1, 4) for s in subs):
        KF = 4 * 128 * T
        VF = 8 * 128 * 512
        SF = 128 * 4
        kvc = (nc.dram_tensor("kvcK", [1, KF + SF], BF16, kind="Internal").ap(),
               nc.dram_tensor("kvcV", [1, VF], BF16, kind="Internal").ap())
        kva = (nc.dram_tensor("kvaK", [2, KF + SF], BF16, kind="Internal").ap(),
               nc.dram_tensor("kvaV", [2, VF], BF16, kind="Internal").ap())

    A = {}
    def sbuf(name, shape, dt):
        A[name] = nc.alloc_sbuf_tensor(name, list(shape), dt)

    sbuf("ax", [128, KT * T], F32)
    sbuf("axn", [128, KT * T], BF16)
    sbuf("aq", [128, KT * T], BF16)
    sbuf("ak", [128, 4 * 2048], BF16)
    sbuf("av", [128, 16 * 512], BF16)
    sbuf("aw", [128, 8192], BF16)
    sbuf("awf", [128, 1024], F32)
    sbuf("at", [128, 2560], F32)
    sbuf("atb", [128, 1024], BF16)
    sbuf("amask", [128, 2560], BF16)
    sbuf("acs", [128, 2048], BF16)
    sbuf("asm", [128, 88], F32)
    sbuf("cn", [128, NCONST], F32)
    sbuf("pos", [128, 2], F32)
    sbuf("onesq", [128, 128], BF16)
    sbuf("onesqf", [128, 128], F32)
    sbuf("identf", [128, 128], F32)

    sbuf("echo_sb", [1, 16], F32)
    P_echo = lambda e: e.dma_start(A["echo_sb"][:, :], echo_dram)
    PS = [es.enter_context(nc.psum_tensor(f"pb{i}", [128, 512], F32)) for i in range(7)]
    X.PT = es.enter_context(nc.psum_tensor("pt", [128, 128], F32))

    P = Prog()
    P.echo_fn = P_echo
    X.P, X.A, X.PS = P, A, PS
    X.w_in = w_in

    X.xt = lambda kt: A["ax"][:, kt * T:(kt + 1) * T]
    X.xc = lambda kt, c: A["ax"][:, kt * T + c * 512: kt * T + (c + 1) * 512]
    X.xnt = lambda kt: A["axn"][:, kt * T:(kt + 1) * T]
    X.xnc = lambda kt, c: A["axn"][:, kt * T + c * 512: kt * T + (c + 1) * 512]
    X.cnv = lambda name, i: A["cn"][:, CONST_OFF[name] + i: CONST_OFF[name] + i + 1]
    X.wreg = lambda i: A["aw"][:, i * 2048:(i + 1) * 2048]

    nc.const_aps.register(F32, EPS, A["cn"][:, CONST_OFF["eps"]:CONST_OFF["eps"] + 1])
    X.bank_i = 0
    def next_bank():
        i = X.bank_i % 7
        X.bank_i += 1
        return PS[i], ("ps", i)
    X.next_bank = next_bank

    P.dma("io", "sync", lambda e: e.dma_start(A["cn"][:, :], consts_in), writes=[("cn",)], echo=False)
    P.dma("io", "sync", lambda e: e.dma_start(A["pos"][:, :], pos_in), writes=[("pos",)], echo=False)
    for c in range(C2):
        P.dma("io", "sync", lambda e, c=c: e.dma_start(
            A["ax"][:, :].rearrange("p (kt t) -> p kt t", kt=KT)[:, :, c * 512:(c + 1) * 512],
            x_in.rearrange("(kt p) t -> p kt t", p=128)[:, :, c * 512:(c + 1) * 512]),
            writes=[("x", c)], echo=False)
    P.op("dve", lambda e: e.memset(A["onesq"][:, :], 1.0), writes=[("onesq",)])
    P.op("dve", lambda e: e.memset(A["onesqf"][:, :], 1.0), writes=[("onesqf",)])
    P.op("gpc", lambda e: e.memset(A["identf"][:, :], 1.0), writes=[("identf",)])
    P.op("gpc", lambda e: e.affine_select(
        out=A["identf"][:, :], in_=A["identf"][:, :], compare_op=ALU.is_equal,
        fill=0.0, base=0, pattern=[[-1, 128]], channel_multiplier=1),
        reads=[("identf",)], writes=[("identf",)])
    if any(s in (0, 2) for s in subs):
        o = CONST_OFF["scratch"]
        sd = CONST_OFF["l0_sd"]
        P.op("dve", lambda e, o=o, sd=sd: e.tensor_single_scalar(
            A["cn"][:, o:o + 16], A["cn"][:, sd:sd + 16], 0.5, ALU.mult),
            reads=[("cn",)], writes=[("cn",)])
    X.SD2 = CONST_OFF["scratch"]

    for s in subs:
        if s == 99:
            emit_ln(X)
        elif s in (0, 2):
            emit_ffn0(X)
        elif s == 1:
            emit_attn(X, 0, kvc, kva)
        elif s == 4:
            emit_attn(X, 1, kvc, kva)
        else:
            emit_ffn1(X)

    P.dma("io", "sync", lambda e: e.dma_start(
        x_out.rearrange("(kt p) t -> p kt t", p=128),
        A["ax"][:, :].rearrange("p (kt t) -> p kt t", kt=KT)),
        reads=[("x", 0), ("x", 1)], echo=False)

    if DEBUG_TAPS:
        for nm, dt_, keys in [("axn", BF16, [("xn", kt) for kt in range(KT)] + [("e", 0), ("e", 1)]),
                              ("aq", BF16, [("fu", j) for j in range(KT)] +
                               [("qa", h, c) for h in range(KT) for c in range(C2)]),
                              ("at", F32, [("at", i) for i in range(6)]),
                              ("ak", BF16, [("kt",)]), ("av", BF16, [("vt",)])]:
            live = [k for k in keys if k in P.bufs]
            if not live:
                continue
            dbg = nc.dram_tensor("dbg_" + nm, [128, A[nm].shape[1]], dt_,
                                 kind="ExternalOutput").ap()
            P.dma("io", "gpsimd", lambda e, dbg=dbg, nm=nm: e.dma_start(dbg, A[nm][:, :]),
                 reads=live, echo=False)

    P.finalize_and_emit(nc, es)
    return nc, X.in_names


def emit_ln(X):
    P, A = X.P, X.A
    atb, at = A["atb"], A["at"]
    P.retag([("xn", kt, c) for kt in range(KT) for c in range(C2)],
            [("e", pp, pt) for pp in range(2) for pt in range(16)])
    s1 = [X.next_bank() for _ in range(C2)]
    s2 = [X.next_bank() for _ in range(C2)]
    for c in range(C2):
        sq = atb[:, c * 512:(c + 1) * 512]
        for kt in range(KT):
            P.op("dve", lambda e, kt=kt, c=c: e.tensor_copy(X.xnc(kt, c), X.xc(kt, c)),
                 reads=[("x", c)], writes=[("xn", kt, c)])
            P.op("dve", lambda e, kt=kt, c=c, sq=sq: e.tensor_mul(
                sq, X.xnc(kt, c), X.xnc(kt, c)),
                reads=[("xn", kt, c)], writes=[("atb", c)])
            P.op("pe", lambda e, kt=kt, c=c: e.matmul(
                s1[c][0][:, :], A["onesq"][:, :], X.xnc(kt, c),
                start=(kt == 0), stop=(kt == KT - 1)),
                reads=[("xn", kt, c), ("onesq",)], writes=[s1[c][1]])
            P.op("pe", lambda e, kt=kt, c=c, sq=sq: e.matmul(
                s2[c][0][:, :], A["onesq"][:, :], sq,
                start=(kt == 0), stop=(kt == KT - 1)),
                reads=[("atb", c), ("onesq",)], writes=[s2[c][1]])
    for c in range(C2):
        mu = at[:, c * 512:(c + 1) * 512]
        r = at[:, 1024 + c * 512: 1024 + (c + 1) * 512]
        sc = at[:, 2048:2560]
        P.op("dve", lambda e, c=c, mu=mu: e.tensor_single_scalar(
            mu, s1[c][0][:, :], 1.0 / D, ALU.mult),
            reads=[s1[c][1]], writes=atk(c * 512, (c + 1) * 512))
        P.op("dve", lambda e, mu=mu, sc=sc: e.tensor_mul(sc, mu, mu),
             reads=atk(c * 512, (c + 1) * 512), writes=atk(2048, 2560))
        P.op("dve", lambda e, c=c, sc=sc: e.scalar_tensor_tensor(
            sc, s2[c][0][:, :], 1.0 / D, sc, ALU.mult, ALU.subtract),
            reads=[s2[c][1]] + atk(2048, 2560), writes=atk(2048, 2560))
        P.op("act", lambda e, sc=sc: e.activation(sc, sc, AF.Ln, bias=EPS, scale=1.0),
             reads=atk(2048, 2560) + [("cn",)], writes=atk(2048, 2560))
        P.op("act", lambda e, r=r, sc=sc: e.activation(r, sc, AF.Exp, bias=0.0, scale=-0.5),
             reads=atk(2048, 2560), writes=atk(1024 + c * 512, 1024 + (c + 1) * 512))
        tmp = at[:, 2048:2560]
        for kt in range(KT):
            P.op("dve", lambda e, kt=kt, c=c, mu=mu, tmp=tmp: e.tensor_sub(
                tmp, X.xc(kt, c), mu),
                reads=[("x", c)] + atk(c * 512, (c + 1) * 512), writes=atk(2048, 2560))
            P.op("dve", lambda e, kt=kt, c=c, r=r, tmp=tmp: e.tensor_mul(
                X.xnc(kt, c), tmp, r),
                reads=atk(2048, 2560) + atk(1024 + c * 512, 1024 + (c + 1) * 512),
                writes=[("xn", kt, c)])


def emit_ffn0(X):
    P, A = X.P, X.A
    emit_ln(X)
    P.retag([("fu", j) for j in range(KT)],
            [("qa", h, c) for h in range(KT) for c in range(C2)])
    aq, at = A["aq"], A["at"]
    wg, wu, wd = X.w_in["l0_wg"], X.w_in["l0_wu"], X.w_in["l0_wd"]
    NG, GT = 4, 16
    for g in range(NG):
        for j in range(GT):
            mt = g * GT + j
            bi = mt % 2
            wgv, kg = X.wreg(bi), ("w", bi)
            wuv, ku = X.wreg(2 + bi), ("w", 2 + bi)
            P.dma(f"w{bi}", "gpsimd", lambda e, mt=mt, wgv=wgv: e.dma_start(wgv, wg[mt]), writes=[kg])
            P.dma(f"w{2+bi}", "gpsimd", lambda e, mt=mt, wuv=wuv: e.dma_start(wuv, wu[mt]), writes=[ku])
            for c in range(C2):
                bg, bgk = X.next_bank()
                bu, buk = X.next_bank()
                mm_accum(P, bg[:, :], bgk,
                         [(wgv[:, kt * 128:(kt + 1) * 128], X.xnc(kt, c), [kg, ("xn", kt, c)])
                          for kt in range(KT)])
                mm_accum(P, bu[:, :], buk,
                         [(wuv[:, kt * 128:(kt + 1) * 128], X.xnc(kt, c), [ku, ("xn", kt, c)])
                          for kt in range(KT)])
                pp = (2 * j + c) % 2
                tg = at[:, pp * 1024: pp * 1024 + 512]
                tu = at[:, pp * 1024 + 512: pp * 1024 + 1024]
                P.op("act", lambda e, bg=bg, tg=tg, mt=mt: e.activation(
                    tg, bg[:, :], AF.Silu, bias=0.0, scale=X.cnv("l0_sg", mt)),
                    reads=[bgk, ("cn",)], writes=atk(pp * 1024, pp * 1024 + 512))
                P.op("dve", lambda e, bu=bu, tu=tu, mt=mt: e.tensor_single_scalar(
                    tu, bu[:, :], X.cnv("l0_su", mt), ALU.mult),
                    reads=[buk, ("cn",)], writes=atk(pp * 1024 + 512, pp * 1024 + 1024))
                P.op("dve", lambda e, tg=tg, tu=tu, j=j, c=c: e.tensor_mul(
                    aq[:, j * T + c * 512: j * T + (c + 1) * 512], tg, tu),
                    reads=atk(pp * 1024, pp * 1024 + 1024), writes=[("fu", j)])
        for mo in range(KT):
            bi = mo % 2
            wdv, kd = X.wreg(bi), ("w", bi)
            P.dma(f"w{bi}", "gpsimd", lambda e, mo=mo, g=g, wdv=wdv: e.dma_start(
                wdv, wd[mo][:, g * 2048:(g + 1) * 2048]), writes=[kd])
            for c in range(C2):
                bd, bdk = X.next_bank()
                mm_accum(P, bd[:, :], bdk,
                         [(wdv[:, jj * 128:(jj + 1) * 128],
                           aq[:, jj * T + c * 512: jj * T + (c + 1) * 512],
                           [kd, ("fu", jj)]) for jj in range(GT)])
                if DEBUG_TAPS and g == 3 and mo < 3 and c == 0:
                    P.op("dve", lambda e, bd=bd, mo=mo: e.tensor_copy(
                        A["av"][:, mo * 512:(mo + 1) * 512], bd[:, :]),
                        reads=[bdk], writes=[("vt",)])
                    P.op("dve", lambda e, wdv=wdv, mo=mo: e.tensor_copy(
                        A["ak"][:, mo * 2048:(mo + 1) * 2048], wdv),
                        reads=[kd], writes=[("kt",)])
                P.op("dve", lambda e, bd=bd, mo=mo, c=c: e.scalar_tensor_tensor(
                    X.xc(mo, c), bd[:, :], A["cn"][:, X.SD2 + mo: X.SD2 + mo + 1],
                    X.xc(mo, c), ALU.mult, ALU.add),
                    reads=[bdk, ("cn",), ("x", c)], writes=[("x", c)])
                if DEBUG_TAPS and mo == 0 and c == 0:
                    P.op("dve", lambda e, g=g: e.tensor_copy(
                        A["av"][:, (4 + g) * 512:(5 + g) * 512], X.xc(0, 0)),
                        reads=[("x", 0)], writes=[("vt",)])


def emit_attn(X, layer, kvc, kva):
    P, A = X.P, X.A
    aw, at, atb, aq, ak, av, asm = (A["aw"], A["at"], A["atb"], A["aq"],
                                    A["ak"], A["av"], A["asm"])
    amask = A["amask"]
    pfx = "l1_" if layer else "l0_"
    wq, wk, wvk, wo = (X.w_in[pfx + "wq"], X.w_in[pfx + "wk"],
                       X.w_in[pfx + "wvk"], X.w_in[pfx + "wo"])
    weng = "gpsimd"

    if layer:
        emit_cossin(X)
    emit_ln(X)
    P.retag([("qa", h, c) for h in range(KT) for c in range(C2)],
            [("fu", j) for j in range(KT)])
    P.retag([("ksq", 0), ("ksq", 1)], [("w", 2)])

    bvb = A["cn"][:, CONST_OFF[pfx + "bvb"]:CONST_OFF[pfx + "bvb"] + 512]
    KS = lambda h2: aw[:, 4096 + h2 * 1024: 4096 + (h2 + 1) * 1024]

    def kstat(mk):
        h2 = mk % 2
        ks = KS(h2)
        sA = asm[:, 8 + 2 * h2: 9 + 2 * h2]
        sB = asm[:, 9 + 2 * h2: 10 + 2 * h2]
        bs = [X.next_bank() for _ in range(C2)]
        for c in range(C2):
            P.op("pe", lambda e, b=bs[c][0], ks=ks, c=c: e.matmul(
                b[:, :], A["onesq"][:, :], ks[:, c * 512:(c + 1) * 512],
                start=True, stop=True), reads=[("ksq", h2), ("onesq",)], writes=[bs[c][1]])
        P.op("dve", lambda e, sA=sA: e.reduce_max(sA, bs[0][0][:, :], axis=AX),
             reads=[bs[0][1]], writes=[("k2m", 2 * h2)])
        P.op("dve", lambda e, sB=sB: e.reduce_max(sB, bs[1][0][:, :], axis=AX),
             reads=[bs[1][1]], writes=[("k2m", 2 * h2 + 1)])
        P.op("dve", lambda e, mk=mk, sA=sA, sB=sB: e.tensor_max(
            asm[:, mk:mk + 1], sA, sB),
            reads=[("k2m", 2 * h2), ("k2m", 2 * h2 + 1)], writes=[("mk2own", mk)])

    pend = None
    for mk in range(4):
        bi = mk % 2
        wv_, kw = X.wreg(bi), ("w", bi)
        P.dma(f"w{bi}", weng, lambda e, mk=mk, wv_=wv_: e.dma_start(wv_, wk[mk]), writes=[kw])
        for c in range(C2):
            b, bk = X.next_bank()
            mm_accum(P, b[:, :], bk,
                     [(wv_[:, kt * 128:(kt + 1) * 128], X.xnc(kt, c), [kw, ("xn", kt, c)])
                      for kt in range(KT)])
            P.op("dve", lambda e, b=b, mk=mk, c=c: e.tensor_single_scalar(
                ak[:, mk * T + c * 512: mk * T + (c + 1) * 512], b[:, :],
                X.cnv(pfx + "bk", mk), ALU.add),
                reads=[bk, ("cn",)], writes=[("kt",)])
        P.op("dve", lambda e, mk=mk: e.tensor_mul(
            KS(mk % 2), ak[:, mk * T:(mk + 1) * T], ak[:, mk * T:(mk + 1) * T]),
            reads=[("kt",)], writes=[("ksq", mk % 2)])
        if pend is not None:
            pend()
        pend = (lambda mk=mk: kstat(mk))
        if layer:
            emit_rope(X, ak[:, mk * T:(mk + 1) * T], [("kt",)])
    pend()
    P.op("dve", lambda e: e.tensor_copy(amask[:, SOFF:SOFF + 4], asm[:, 0:4]),
         reads=[("mk2own", mk) for mk in range(4)], writes=[("mk2b",)])
    KF = 4 * 128 * T
    VF = 8 * 128 * 512
    kvcK, kvcV = kvc
    kvaK, kvaV = kva
    P.dma("kv", "sync", lambda e: e.dma_start(
        kvcK[0:1, 0:KF].rearrange("r (kv p t) -> (r p) kv t", kv=4, p=128),
        ak[:, 0:4096].rearrange("p (kv t) -> p kv t", kv=4)),
        reads=[("kt",)], writes=[("kvcK",)])
    P.dma("kv", "sync", lambda e: e.dma_start(
        kvcK[0:1, KF:].rearrange("r (p c) -> (r p) c", p=128),
        amask[:, SOFF:SOFF + 4]),
        reads=[("mk2b",)], writes=[("kvcK",)])
    P.dma("w0", weng, lambda e: e.dma_start(
        aw[:, 0:8192].rearrange("p (kt d) -> p kt d", kt=16),
        wvk.rearrange("kt p d -> p kt d")),
        writes=[("w", 0), ("w", 1), ("w", 2), ("w", 3), ("ksq", 0), ("ksq", 1)])
    P.op("gpc", lambda e: e.collective_compute(
        "AllGather", ALU.bypass, replica_groups=RG_PAIRS,
        ins=[kvcK.opt()], outs=[kvaK.opt()]),
        reads=[("kvcK",)], writes=[("kvaK",)])
    for ptl in range(8):
        b, bk = X.next_bank()
        mm_accum(P, b[:, :], bk,
                 [(X.xnt(kt)[:, ptl * 128:(ptl + 1) * 128], aw[:, kt * 512:(kt + 1) * 512],
                   [("xn", kt, ptl // 4), ("w", kt // 4)] +
                   ([("ksq", 0)] if kt in (8, 9) else [("ksq", 1)] if kt in (10, 11) else []))
                  for kt in range(KT)])
        P.op("dve", lambda e, b=b, ptl=ptl: e.tensor_add(
            av[:, ptl * 512:(ptl + 1) * 512], b[:, :], bvb),
            reads=[bk, ("cn",)], writes=[("vt",)])
    P.dma("kv", "sync", lambda e: e.dma_start(
        kvcV[0:1, :].rearrange("r (vt p d) -> (r p) vt d", vt=8, p=128),
        av[:, 0:4096].rearrange("p (vt d) -> p vt d", vt=8)),
        reads=[("vt",)], writes=[("kvcV",)])

    def emit_vcoll_readback():
        P.op("gpc", lambda e: e.collective_compute(
            "AllGather", ALU.bypass, replica_groups=RG_PAIRS,
            ins=[kvcV.opt()], outs=[kvaV.opt()]),
            reads=[("kvcV",)], writes=[("kvaV",)])
        for r in range(2):
            P.dma("kv", "sync", lambda e, r=r: e.dma_start(
                amask[:, SOFF + 4 + 4 * r: SOFF + 8 + 4 * r],
                kvaK[r:r + 1, KF:].rearrange("r (p c) -> (r p) c", p=128)),
                reads=[("kvaK",)], writes=[("mk2r", r)])
        for r in range(2):
            for kv4 in range(4):
                P.dma("kv", "sync", lambda e, r=r, kv4=kv4: e.dma_start(
                    ak[:, kv4 * 2048 + r * T: kv4 * 2048 + (r + 1) * T],
                    kvaK[r:r + 1, kv4 * 128 * T:(kv4 + 1) * 128 * T].rearrange(
                        "r (p t) -> (r p) t", p=128)),
                    reads=[("kvaK",)], writes=[("kt",)])
        for r in range(2):
            P.dma("kv", "sync", lambda e, r=r: e.dma_start(
                av[:, r * 4096:(r + 1) * 4096].rearrange("p (vt d) -> p vt d", vt=8),
                kvaV[r:r + 1, :].rearrange("r (vt p d) -> (r p) vt d", vt=8, p=128)),
                reads=[("kvaV",)], writes=[("vt",)])

    def qstat(mq):
        h2 = mq % 2
        qs = KS(h2)
        for qc in range(C2):
            b, bk = X.next_bank()
            P.op("pe", lambda e, b=b, qs=qs, qc=qc: e.matmul(
                b[:, :], A["onesq"][:, :], qs[:, qc * 512:(qc + 1) * 512],
                start=True, stop=True), reads=[("ksq", h2), ("onesq",)], writes=[bk])
            P.op("dve", lambda e, b=b, mq=mq, qc=qc: e.reduce_max(
                asm[:, 12 + mq * 2 + qc: 13 + mq * 2 + qc], b[:, :], axis=AX),
                reads=[bk], writes=[("nb", mq, qc)])

    pend = None
    for mq in range(KT):
        bi = mq % 2
        wv_, kw = X.wreg(bi), ("w", bi)
        P.dma(f"w{bi}", weng, lambda e, mq=mq, wv_=wv_: e.dma_start(wv_, wq[mq]), writes=[kw])
        for c in range(C2):
            b, bk = X.next_bank()
            mm_accum(P, b[:, :], bk,
                     [(wv_[:, kt * 128:(kt + 1) * 128], X.xnc(kt, c), [kw, ("xn", kt, c)])
                      for kt in range(KT)])
            P.op("dve", lambda e, b=b, mq=mq, c=c: e.tensor_single_scalar(
                aq[:, mq * T + c * 512: mq * T + (c + 1) * 512], b[:, :],
                X.cnv(pfx + "bq", mq), ALU.add),
                reads=[bk, ("cn",)], writes=[("qa", mq, c)])
        P.op("dve", lambda e, mq=mq: e.tensor_mul(
            KS(mq % 2), aq[:, mq * T:(mq + 1) * T], aq[:, mq * T:(mq + 1) * T]),
            reads=[("qa", mq, 0), ("qa", mq, 1)], writes=[("ksq", mq % 2)])
        if pend is not None:
            pend()
        pend = (lambda mq=mq: qstat(mq))
        if layer:
            emit_rope(X, aq[:, mq * T:(mq + 1) * T], [("qa", mq, 0), ("qa", mq, 1)])
        if mq == 1:
            emit_vcoll_readback()
    pend()
    P.retag([("w", 2)], [("ksq", 0), ("ksq", 1)])
    P.op("dve", lambda e: e.tensor_max(
        amask[:, SOFF + 4:SOFF + 8], amask[:, SOFF + 4:SOFF + 8],
        amask[:, SOFF + 8:SOFF + 12]),
        reads=[("mk2r", 0), ("mk2r", 1)], writes=[("mk2r", 0)])
    P.op("dve", lambda e: e.tensor_copy(asm[:, 0:4], amask[:, SOFF + 4:SOFF + 8]),
         reads=[("mk2r", 0)], writes=[("mk2f",)])
    negB = asm[:, 12:44]
    for kv in range(4):
        blk = asm[:, 12 + kv * 8: 20 + kv * 8]
        P.op("dve", lambda e, blk=blk, kv=kv: e.tensor_single_scalar(
            blk, blk, asm[:, kv:kv + 1], ALU.mult),
            reads=[("nb", kv * 4 + g, qc) for g in range(4) for qc in range(C2)]
            + [("mk2f",)], writes=[("nbk", kv)])
    P.op("act", lambda e: e.activation(negB, negB, AF.Sqrt, bias=0.0, scale=1.0),
         reads=[("nbk", kv) for kv in range(4)], writes=[("negb",)])
    P.op("dve", lambda e: e.tensor_single_scalar(negB, negB, -SCALE, ALU.mult),
         reads=[("negb",)], writes=[("negb",)])
    P.retag([("e", pp, pt) for pp in range(2) for pt in range(16)],
            [("xn", kt, c) for kt in range(KT) for c in range(C2)])
    SC = (0, 1, 2)
    evt = lambda pp, pt: A["axn"][:, pp * 8192 + pt * 512: pp * 8192 + (pt + 1) * 512]

    def fin(st):
        h, qc, pp, rsb, pvb, kv, s = st[:7]
        par = s % 2
        rec = at[:, 2048:2560] if par == 0 else at[:, 0:512]
        rk = atk(2048, 2560) if par == 0 else atk(0, 512)
        P.op("dve", lambda e, rec=rec, rsb=rsb: e.tensor_single_scalar(
            rec, X.PS[rsb][:, :], 1e-30, ALU.add), reads=[("ps", rsb)], writes=rk)
        P.op("dve", lambda e, rec=rec: e.reciprocal(rec, rec), reads=rk, writes=rk)
        P.op("dve", lambda e, rec=rec, pvb=pvb, h=h, qc=qc: e.tensor_mul(
            aq[:, h * T + qc * 512: h * T + (qc + 1) * 512], X.PS[pvb][:, :], rec),
            reads=[("ps", pvb)] + rk, writes=[("qa", h, qc)])

    def rs_pv_part(st, idx):
        ph, pqc, ppp, prsb, ppvb, pkv, ps_, ptiles = st
        if idx >= len(ptiles):
            return
        pt = ptiles[idx]
        last = idx == len(ptiles) - 1
        P.op("pe", lambda e, prsb=prsb, ppp=ppp, pt=pt, f=(idx == 0), l=last: e.matmul(
            X.PS[prsb][:, :], A["onesq"][:, :], evt(ppp, pt),
            start=f, stop=l),
            reads=[("onesq",), ("e", ppp, pt)], writes=[("ps", prsb)])
        P.op("pe", lambda e, ppvb=ppvb, ppp=ppp, pkv=pkv, pt=pt, f=(idx == 0), l=last: e.matmul(
            X.PS[ppvb][:, :],
            av[:, pt * 512 + pkv * 128: pt * 512 + (pkv + 1) * 128],
            evt(ppp, pt),
            start=f, stop=l),
            reads=[("vt",), ("e", ppp, pt)], writes=[("ps", ppvb)])

    prev = None
    s = 0
    for qc in range(C2):
        tiles = LIVE_T[qc] if layer else list(range(16))
        masked = MASK_T[qc] if layer else ()
        if layer:
            emit_mask(X, qc)
        for kv in range(4):
            for g in range(4):
                h = kv * 4 + g
                pp = s % 2
                rsb, pvb = 3 + (s % 2), 5 + (s % 2)
                for idx, pt in enumerate(tiles):
                    sb = SC[idx % 3]
                    P.op("pe", lambda e, sb=sb, kv=kv, pt=pt, h=h, qc=qc: e.matmul(
                        X.PS[sb][:, :],
                        ak[:, kv * 2048 + pt * 128: kv * 2048 + (pt + 1) * 128],
                        aq[:, h * T + qc * 512: h * T + (qc + 1) * 512],
                        start=True, stop=True),
                        reads=[("kt",), ("qa", h, qc)], writes=[("ps", sb)])
                    P.op("act", lambda e, sb=sb, pp=pp, pt=pt, h=h, qc=qc: e.activation(
                        evt(pp, pt), X.PS[sb][:, :], AF.Exp,
                        bias=negB[:, h * 2 + qc: h * 2 + qc + 1], scale=SCALE),
                        reads=[("ps", sb), ("negb",)], writes=[("e", pp, pt)])
                    if pt in masked:
                        P.op("dve", lambda e, pp=pp, pt=pt: e.tensor_mul(
                            evt(pp, pt), evt(pp, pt),
                            amask[:, (pt // 8) * 896 + 384 - (pt % 4) * 128:
                                  (pt // 8) * 896 + 896 - (pt % 4) * 128]),
                            reads=[("e", pp, pt), ("mask",)], writes=[("e", pp, pt)])
                    if prev is not None:
                        rs_pv_part(prev, idx)
                if prev is not None:
                    for idx2 in range(len(tiles), len(prev[7])):
                        rs_pv_part(prev, idx2)
                    fin(prev)
                prev = (h, qc, pp, rsb, pvb, kv, s, tiles)
                s += 1
    for idx in range(len(prev[7])):
        rs_pv_part(prev, idx)
    fin(prev)
    for mo in range(KT):
        bi = mo % 2
        wv_, kw = X.wreg(bi), ("w", bi)
        P.dma(f"w{bi}", weng, lambda e, mo=mo, wv_=wv_: e.dma_start(wv_, wo[mo]), writes=[kw])
        for c in range(C2):
            b, bk = X.next_bank()
            mm_accum(P, b[:, :], bk,
                     [(wv_[:, kt * 128:(kt + 1) * 128],
                       aq[:, kt * T + c * 512: kt * T + (c + 1) * 512],
                       [kw, ("qa", kt, c)]) for kt in range(KT)])
            if layer == 0:
                P.op("dve", lambda e, b=b, mo=mo, c=c: e.scalar_tensor_tensor(
                    X.xc(mo, c), b[:, :], X.cnv("l0_bo", mo), X.xc(mo, c),
                    ALU.add, ALU.add),
                    reads=[bk, ("cn",), ("x", c)], writes=[("x", c)])
            else:
                P.op("dve", lambda e, b=b, mo=mo, c=c: e.tensor_add(
                    X.xc(mo, c), X.xc(mo, c), b[:, :]),
                    reads=[bk, ("x", c)], writes=[("x", c)])

def emit_rope(X, tile, keys):
    P, A = X.P, X.A
    atb, acs = A["atb"], A["acs"]
    rot = atb[:, 0:1024]
    cosb = acs[:, 0:1024]
    sinb = acs[:, 1024:2048]
    P.dma("io", "scalar", lambda e: e.dma_start(rot[0:64, :], tile[64:128, :]),
          reads=list(keys), writes=[("atb", 0), ("atb", 1)])
    P.dma("io", "scalar", lambda e: e.dma_start(rot[64:128, :], tile[0:64, :]),
          reads=list(keys), writes=[("atb", 0), ("atb", 1)])
    P.op("dve", lambda e: e.tensor_single_scalar(rot[0:64, :], rot[0:64, :], -1.0, ALU.mult),
         reads=[("atb", 0), ("atb", 1)], writes=[("atb", 0), ("atb", 1)])
    P.op("dve", lambda e: e.tensor_mul(tile, tile, cosb),
         reads=list(keys) + [("cs",)], writes=list(keys))
    P.op("dve", lambda e: e.tensor_mul(rot, rot, sinb),
         reads=[("atb", 0), ("atb", 1), ("cs",)], writes=[("atb", 0), ("atb", 1)])
    P.op("dve", lambda e: e.tensor_add(tile, tile, rot),
         reads=list(keys) + [("atb", 0), ("atb", 1)], writes=list(keys))

def emit_cossin(X):
    P, A = X.P, X.A
    at = A["at"]
    tg = at[:, 0:1024]
    sc = at[:, 1024:2048]
    sci = sc.bitcast(I32)
    iv = A["asm"][:, 48:49]
    P.op("gpc", lambda e: e.iota(iv, [[0, 1]], channel_multiplier=1,
                                 allow_small_or_imprecise_dtypes=True), writes=[("iv",)])
    iv2 = A["asm"][:, 52:53]
    P.op("dve", lambda e, iv2=iv2: e.tensor_single_scalar(iv2, iv, 64.0, ALU.is_ge),
         reads=[("iv",)], writes=[("iv2",)])
    P.op("dve", lambda e, iv2=iv2: e.scalar_tensor_tensor(iv, iv2, -64.0, iv, ALU.mult, ALU.add),
         reads=[("iv",), ("iv2",)], writes=[("iv",)])
    P.op("act", lambda e: e.activation(iv, iv, AF.Exp, bias=0.0,
                                       scale=float(-np.log(1e6) / 64.0)),
         reads=[("iv",)], writes=[("iv",)])
    P.op("gpc", lambda e: e.iota(tg, [[1, 1024]], channel_multiplier=0,
                                 allow_small_or_imprecise_dtypes=True),
         writes=atk(0, 1024))
    P.op("dve", lambda e: e.tensor_single_scalar(
        tg[:, 0:512], tg[:, 0:512], A["pos"][:, 0:1], ALU.add),
        reads=atk(0, 1024) + [("pos",)], writes=atk(0, 1024))
    P.op("dve", lambda e: e.tensor_single_scalar(
        tg[:, 512:1024], tg[:, 512:1024], A["pos"][:, 1:2], ALU.add),
        reads=atk(0, 1024) + [("pos",)], writes=atk(0, 1024))
    P.op("dve", lambda e: e.tensor_single_scalar(
        tg[:, 512:1024], tg[:, 512:1024], -512.0, ALU.add),
        reads=atk(0, 1024), writes=atk(0, 1024))
    P.op("dve", lambda e: e.tensor_single_scalar(tg, tg, iv, ALU.mult),
         reads=atk(0, 1024) + [("iv",)], writes=atk(0, 1024))
    for off, shift in ((1024, 0.0), (0, float(np.pi / 2))):
        P.op("dve", lambda e: e.tensor_single_scalar(
            sc, tg, float(1.0 / (2 * np.pi)), ALU.mult),
            reads=atk(0, 1024), writes=atk(1024, 2048))
        if shift:
            P.op("dve", lambda e, shift=shift: e.tensor_single_scalar(
                sc, sc, float(shift / (2 * np.pi)), ALU.add),
                reads=atk(1024, 2048), writes=atk(1024, 2048))
        P.op("dve", lambda e: e.tensor_copy(sci, sc),
             reads=atk(1024, 2048), writes=atk(1024, 2048))
        P.op("dve", lambda e: e.tensor_copy(sc, sci),
             reads=atk(1024, 2048), writes=atk(1024, 2048))
        P.op("dve", lambda e: e.scalar_tensor_tensor(
            sc, sc, float(-2 * np.pi), tg, ALU.mult, ALU.add),
            reads=atk(0, 2048), writes=atk(1024, 2048))
        if shift:
            P.op("dve", lambda e, shift=shift: e.tensor_single_scalar(
                sc, sc, float(shift), ALU.add),
                reads=atk(1024, 2048), writes=atk(1024, 2048))
        P.op("act", lambda e, off=off: e.activation(
            A["acs"][:, off:off + 1024], sc, AF.Sin, bias=0.0, scale=1.0),
            reads=atk(1024, 2048), writes=[("cs",)])

def emit_mask(X, qc):
    P, A = X.P, X.A
    P.retag([("mask",)], [("sel",), ("wet",)])
    P.op("gpc", lambda e: e.iota(A["at"][:, 0:896], [[1, 896]], channel_multiplier=-1,
                                 allow_small_or_imprecise_dtypes=True),
         writes=atk(0, 896))
    strips = sorted({pt // 4 for pt in MASK_T[qc]})
    for kc in strips:
        th = A["asm"][:, 84 + kc: 85 + kc]
        P.op("dve", lambda e, qc=qc, kc=kc, th=th: e.tensor_scalar(
            th, A["pos"][:, qc:qc + 1], -1.0, float(384 + KCB[kc]), ALU.mult, ALU.add),
            reads=[("pos",)], writes=[("mth", kc)])
        sl = kc >> 1
        P.op("dve", lambda e, sl=sl, th=th: e.tensor_single_scalar(
            A["amask"][:, sl * 896:(sl + 1) * 896], A["at"][:, 0:896], th, ALU.is_ge),
            reads=atk(0, 896) + [("mth", kc), ("mk2r", 0)], writes=[("mask",)])


def emit_ffn1(X):
    P, A = X.P, X.A
    aw, awf, at, aq, acs, asm = (A["aw"], A["awf"], A["at"], A["aq"],
                                 A["acs"], A["asm"])
    emit_ln(X)
    P.retag([("fu", j) for j in range(KT)],
            [("qa", h, c) for h in range(KT) for c in range(C2)])
    gwp = X.w_in["l1_gw"]
    bcs, bcsk = X.next_bank()
    bz = [X.next_bank() for _ in range(C2)]
    for half in range(2):
        P.dma("w4", "gpsimd", lambda e, half=half: e.dma_start(
            awf[:, :], gwp[0][:, half * 1024:(half + 1) * 1024]), writes=[("wf",)])
        for ktl in range(8):
            kt = half * 8 + ktl
            P.op("pe", lambda e, ktl=ktl, st=(half == 0 and ktl == 0),
                 sp_=(half == 1 and ktl == 7): e.matmul(
                bcs[:, 0:128], A["onesqf"][:, :], awf[:, ktl * 128:(ktl + 1) * 128],
                start=st, stop=sp_), reads=[("onesqf",), ("wf",)], writes=[bcsk])
            for c in range(C2):
                P.op("pe", lambda e, kt=kt, ktl=ktl, c=c,
                     st=(half == 0 and ktl == 0), sp_=(half == 1 and ktl == 7): e.matmul(
                    bz[c][0][:, :], awf[:, ktl * 128:(ktl + 1) * 128], X.xc(kt, c),
                    start=st, stop=sp_), reads=[("wf",), ("x", c)], writes=[bz[c][1]])
    cssb = at[:, 2048:2176]
    P.op("dve", lambda e, bcs=bcs, cssb=cssb: e.tensor_copy(cssb, bcs[:, 0:128]),
         reads=[bcsk], writes=atk(2048, 2176))
    P.op("pe", lambda e, cssb=cssb: e.transpose(X.PT[:, :], cssb, A["identf"][:, :]),
         reads=atk(2048, 2176) + [("identf",)], writes=[("pt",)])
    csc = asm[:, 50:51]
    P.op("dve", lambda e: e.tensor_copy(csc, X.PT[:, 0:1]), reads=[("pt",)], writes=[("csc",)])
    for c in range(C2):
        b, bk = bz[c]
        z = awf[:, c * 512:(c + 1) * 512]
        mu = at[:, c * 512:(c + 1) * 512]
        r = at[:, 1024 + c * 512: 1024 + (c + 1) * 512]
        t1 = at[:, 2048:2560]
        P.op("dve", lambda e, b=b, z=z: e.tensor_copy(z, b[:, :]),
             reads=[bk], writes=[("wf",)])
        P.op("dve", lambda e, mu=mu, t1=t1: e.tensor_single_scalar(t1, mu, csc, ALU.mult),
             reads=atk(c * 512, (c + 1) * 512) + [("csc",)], writes=atk(2048, 2560))
        P.op("dve", lambda e, z=z, t1=t1: e.tensor_sub(z, z, t1),
             reads=[("wf",)] + atk(2048, 2560), writes=[("wf",)])
        P.op("dve", lambda e, z=z, r=r: e.tensor_mul(z, z, r),
             reads=[("wf",)] + atk(1024 + c * 512, 1024 + (c + 1) * 512), writes=[("wf",)])
        P.op("act", lambda e, z=z: e.activation(z, z, AF.Sigmoid, bias=0.0, scale=1.0),
             reads=[("wf",)], writes=[("wf",)])
        for tt in range(4):
            g_tt = c * 4 + tt
            P.op("dve", lambda e, z=z, tt=tt: e.tensor_copy(
                A["atb"][:, 0:256].bitcast(F32)[:, :], z[:, tt * 128:(tt + 1) * 128]),
                reads=[("wf",)], writes=[("wtm",), ("wf",)])
            P.op("pe", lambda e: e.transpose(X.PT[:, :], A["atb"][:, 0:256].bitcast(F32)[:, :], A["identf"][:, :]),
                 reads=[("wtm",), ("identf",)], writes=[("pt",)])
            P.op("dve", lambda e, g_tt=g_tt: e.tensor_copy(
                A["atb"][:, 256:384].bitcast(F32)[:, g_tt * 8:(g_tt + 1) * 8], X.PT[:, 0:8]),
                reads=[("pt",)], writes=[("sct", g_tt), ("wf",)])
    emit_sel8(X)
    gb = A["cn"][:, CONST_OFF["gbias"]:CONST_OFF["gbias"] + 8]
    for tt in range(8):
        sc = A["atb"][:, 256:384].bitcast(F32)[:, tt * 8:(tt + 1) * 8]
        w1 = asm[:, 56:64]
        m1 = asm[:, 64:65]
        m2 = asm[:, 65:66]
        msk = asm[:, 66:74]
        msk2 = asm[:, 74:82]
        wsum = asm[:, 82:83]
        P.op("dve", lambda e, sc=sc: e.tensor_add(w1, sc, gb),
             reads=[("sct", tt), ("cn",)], writes=[("tk1",)])
        P.op("dve", lambda e: e.reduce_max(m1, w1, axis=AX),
             reads=[("tk1",)], writes=[("tkm1",)])
        P.op("dve", lambda e: e.tensor_single_scalar(msk, w1, m1, ALU.is_equal),
             reads=[("tk1",), ("tkm1",)], writes=[("tkk1",)])
        P.op("dve", lambda e: e.scalar_tensor_tensor(w1, msk, -1e30, w1, ALU.mult, ALU.add),
             reads=[("tkk1",), ("tk1",)], writes=[("tk1",)])
        P.op("dve", lambda e: e.reduce_max(m2, w1, axis=AX),
             reads=[("tk1",)], writes=[("tkm2",)])
        P.op("dve", lambda e: e.tensor_single_scalar(msk2, w1, m2, ALU.is_equal),
             reads=[("tk1",), ("tkm2",)], writes=[("tkk2",)])
        P.op("dve", lambda e: e.tensor_add(msk, msk, msk2),
             reads=[("tkk1",), ("tkk2",)], writes=[("tkk1",)])
        P.op("dve", lambda e, sc=sc: e.tensor_mul(msk, msk, sc),
             reads=[("tkk1",), ("sct", tt)], writes=[("tkk1",)])
        P.op("dve", lambda e: e.reduce_sum(wsum, msk, axis=AX),
             reads=[("tkk1",)], writes=[("tks",)])
        P.op("dve", lambda e: e.reciprocal(wsum, wsum), reads=[("tks",)], writes=[("tks",)])
        P.op("dve", lambda e: e.tensor_single_scalar(wsum, wsum, 0.5, ALU.mult),
             reads=[("tks",)], writes=[("tks",)])
        P.op("dve", lambda e: e.tensor_single_scalar(
            A["atb"][:, 0:256].bitcast(F32)[:, 0:8], msk, wsum, ALU.mult),
            reads=[("tkk1",), ("tks",)], writes=[("wtm",), ("wf",)])
        P.op("pe", lambda e: e.transpose(X.PT[:, :], A["atb"][:, 0:256].bitcast(F32)[:, :], A["identf"][:, :]),
             reads=[("wtm",), ("identf",)], writes=[("pt",)])
        P.op("dve", lambda e, tt=tt: e.tensor_copy(
            A["amask"][:, 1024:2048][:, tt * 128:(tt + 1) * 128], X.PT[:, :]),
            reads=[("pt",)], writes=[("wet",)])
    sgw, sdw = X.w_in["l1_sgw"], X.w_in["l1_sdw"]
    for grp in range(2):
        emit_glu_group(
            X, n_j=11,
            g_slab=lambda j, grp=grp: sgw[22 + grp * 11 + j],
            u_slab=lambda j, grp=grp: sgw[grp * 11 + j],
            d_slab=lambda mo, grp=grp: (sdw[mo], grp * 11 * 128, 11 * 128),
            webk=None)
    upw, dnw = X.w_in["l1_up"], X.w_in["l1_dn"]
    webc = acs[:, 0:1024]
    for ex in range(8):
        for c in range(C2):
            bwb, bwbk = X.next_bank()
            P.op("pe", lambda e, bwb=bwb, ex=ex, c=c: e.matmul(
                bwb[:, :], A["amask"][:, 0:1024].rearrange("p (a b) -> p a b", a=1)[:, 0, :] if False else A["amask"][:, 0:1024][:, ex * 128:(ex + 1) * 128],
                A["amask"][:, 1024:2048][:, c * 512:(c + 1) * 512], start=True, stop=True),
                reads=[("sel",), ("wet",)], writes=[bwbk])
            P.op("dve", lambda e, bwb=bwb, c=c: e.tensor_copy(
                webc[:, c * 512:(c + 1) * 512], bwb[:, :]),
                reads=[bwbk], writes=[("cs",)])
        emit_glu_group(
            X, n_j=11,
            g_slab=lambda j, ex=ex: upw[ex][j],
            u_slab=lambda j, ex=ex: upw[ex][11 + j],
            d_slab=lambda mo, ex=ex: (dnw[ex][mo], 0, 11 * 128),
            webk=webc)

def emit_sel8(X):
    P, A = X.P, X.A
    P.retag([("sel",), ("wet",), ("s8p",)], [("mask",)])
    t = A["amask"][:, 1024:2048]
    pi = A["asm"][:, 51:52]
    P.op("gpc", lambda e, t=t: e.iota(t, [[1, 8], [0, 128]], channel_multiplier=0,
                                      allow_small_or_imprecise_dtypes=True),
         writes=[("wet",)])
    P.op("gpc", lambda e, pi=pi: e.iota(pi, [[0, 1]], channel_multiplier=1,
                                        allow_small_or_imprecise_dtypes=True),
         writes=[("s8p",)])
    P.op("dve", lambda e, t=t, pi=pi: e.tensor_single_scalar(
        A["amask"][:, 0:1024], t, pi, ALU.is_equal),
        reads=[("wet",), ("s8p",)], writes=[("sel",)])


def emit_glu_group(X, n_j, g_slab, u_slab, d_slab, webk):
    P, A = X.P, X.A
    aw, at, aq = A["aw"], A["at"], A["aq"]
    for j in range(n_j):
        bi = j % 2
        gv, kg = X.wreg(bi), ("w", bi)
        uv, ku = X.wreg(2 + bi), ("w", 2 + bi)
        P.dma(f"w{bi}", "gpsimd", lambda e, gv=gv, j=j: e.dma_start(gv, g_slab(j)), writes=[kg])
        P.dma(f"w{2+bi}", "gpsimd", lambda e, uv=uv, j=j: e.dma_start(uv, u_slab(j)), writes=[ku])
        for c in range(C2):
            bg, bgk = X.next_bank()
            bu, buk = X.next_bank()
            mm_accum(P, bg[:, :], bgk,
                     [(gv[:, kt * 128:(kt + 1) * 128], X.xnc(kt, c), [kg, ("xn", kt, c)])
                      for kt in range(KT)])
            mm_accum(P, bu[:, :], buk,
                     [(uv[:, kt * 128:(kt + 1) * 128], X.xnc(kt, c), [ku, ("xn", kt, c)])
                      for kt in range(KT)])
            pp = (2 * j + c) % 2
            tg = at[:, pp * 1024: pp * 1024 + 512]
            tu = at[:, pp * 1024 + 512: pp * 1024 + 1024]
            P.op("act", lambda e, bg=bg, tg=tg: e.activation(
                tg, bg[:, :], AF.Silu, bias=0.0, scale=1.0),
                reads=[bgk], writes=atk(pp * 1024, pp * 1024 + 512))
            if webk is None:
                P.op("dve", lambda e, bu=bu, tu=tu: e.tensor_single_scalar(
                    tu, bu[:, :], 0.5, ALU.mult),
                    reads=[buk], writes=atk(pp * 1024 + 512, pp * 1024 + 1024))
            else:
                P.op("dve", lambda e, bu=bu, tu=tu, c=c: e.tensor_mul(
                    tu, bu[:, :], webk[:, c * 512:(c + 1) * 512]),
                    reads=[buk, ("cs",)], writes=atk(pp * 1024 + 512, pp * 1024 + 1024))
            P.op("dve", lambda e, tg=tg, tu=tu, j=j, c=c: e.tensor_mul(
                aq[:, j * T + c * 512: j * T + (c + 1) * 512], tg, tu),
                reads=atk(pp * 1024, pp * 1024 + 1024), writes=[("fu", j)])
    for mo in range(KT):
        dsl, koff, klen = d_slab(mo)
        bi = mo % 2
        dv = X.wreg(2 + bi)[:, 0:klen]
        kd = ("w", 2 + bi)
        P.dma("w4" if bi == 0 else "w5", "gpsimd",
              lambda e, dv=dv, dsl=dsl, koff=koff, klen=klen: e.dma_start(
                  dv, dsl[:, koff:koff + klen]), writes=[kd])
        for c in range(C2):
            bd, bdk = X.next_bank()
            mm_accum(P, bd[:, :], bdk,
                     [(dv[:, jj * 128:(jj + 1) * 128],
                       aq[:, jj * T + c * 512: jj * T + (c + 1) * 512],
                       [kd, ("fu", jj)]) for jj in range(n_j)])
            P.op("dve", lambda e, bd=bd, mo=mo, c=c: e.tensor_add(
                X.xc(mo, c), X.xc(mo, c), bd[:, :]),
                reads=[bdk, ("x", c)], writes=[("x", c)])


def kernel(**inputs):
    p = prep_inputs(inputs)
    xs = p["x_shards"]
    for grp in GROUPS:
        nc, in_names = build_program(grp)
        in_maps = []
        for c in range(8):
            m = {}
            for n in in_names:
                if n == "x_in":
                    m[n] = xs[c]
                elif n == "pos_off":
                    m[n] = p["pos_off"][c]
                else:
                    m[n] = p[n]
            in_maps.append(m)
        res = run_bass_kernel_spmd(nc, in_maps, core_ids=list(range(8)))
        xs = [np.asarray(res.results[c]["x_out"]) for c in range(8)]
    out = np.empty((4, 4, 512, 2048), np.float32)
    for c in range(8):
        xt = xs[c].T
        chs = CHUNKS[c % 2]
        out[c // 2, chs[0]] = xt[0:512]
        out[c // 2, chs[1]] = xt[512:1024]
    return np.ascontiguousarray(out.reshape(4, 2048, 2048))


# version 05
# speedup vs baseline: 1.1883x; 1.1883x over previous
import numpy as np
import ml_dtypes
from contextlib import ExitStack
import concourse.bass as bass
import concourse.mybir as mybir
from concourse.bass_utils import run_bass_kernel_spmd

F32 = mybir.dt.float32
BF16 = mybir.dt.bfloat16
I8 = mybir.dt.int8
I32 = mybir.dt.int32
AF = mybir.ActivationFunctionType
ALU = mybir.AluOpType
AX = mybir.AxisListType.X

T = 1024
D = 2048
KT = D // 128
C2 = 2
SCALE = float(1.0 / np.sqrt(128.0))
EPS = 1e-5
RG_PAIRS = [[0, 1], [2, 3], [4, 5], [6, 7]]

DEBUG_TAPS = False
DETECT_RACES = False

GROUPS = [[0, 1, 2, 3, 4, 5]]


def _np(a):
    return np.asarray(a)

def mslab(w):
    K, M = w.shape
    return np.ascontiguousarray(
        w.reshape(K // 128, 128, M // 128, 128).transpose(2, 1, 0, 3).reshape(M // 128, 128, K))

def pack_feat(v):
    v = _np(v).astype(np.float32)
    return np.ascontiguousarray(v.reshape(-1, 128).T)

_CONST_SPEC = [("l0_bq", 16), ("l0_bk", 4), ("l0_bo", 16), ("l0_sg", 64),
               ("l0_su", 64), ("l0_sd", 16), ("l1_bq", 16), ("l1_bk", 4),
               ("gbias", 8), ("scratch", 16), ("eps", 1), ("l0_bvb", 512), ("l1_bvb", 512)]
CONST_OFF = {}
_c = 0
for _n, _w in _CONST_SPEC:
    CONST_OFF[_n] = _c
    _c += _w
NCONST = _c

CHUNKS = [(0, 3), (1, 2)]
KCB = [0, 1536, 512, 1024]
LIVE_T = {0: [0, 1, 2, 3, 8, 9, 10, 11], 1: list(range(16))}
MASK_T = {0: {0, 1, 2, 3, 8, 9, 10, 11}, 1: {4, 5, 6, 7, 12, 13, 14, 15}}
SOFF = 2048


def prep_inputs(inputs):
    p = {}
    x = _np(inputs["L0_attn_x"]).reshape(4, 4, 512, 2048)
    p["x_shards"] = [
        np.ascontiguousarray(
            np.concatenate([x[c // 2, ch] for ch in CHUNKS[c % 2]], axis=0).T)
        for c in range(8)]

    p["l0_wq"] = mslab(_np(inputs["L0_attn_Wq"]).astype(np.float32))
    p["l0_wk"] = mslab(_np(inputs["L0_attn_Wk"]).astype(np.float32))
    p["l0_wvk"] = np.ascontiguousarray(
        _np(inputs["L0_attn_Wv"]).astype(np.float32).reshape(16, 128, 512))
    p["l0_wo"] = mslab(_np(inputs["L0_attn_Wo"]).astype(np.float32))
    p["l0_wg"] = mslab(_np(inputs["L0_ffn_w_gate"]))
    p["l0_wu"] = mslab(_np(inputs["L0_ffn_w_up"]))
    p["l0_wd"] = mslab(_np(inputs["L0_ffn_w_down"]))

    p["l1_wq"] = mslab(_np(inputs["L1_attn_Wq"]))
    p["l1_wk"] = mslab(_np(inputs["L1_attn_Wk"]))
    p["l1_wvk"] = np.ascontiguousarray(_np(inputs["L1_attn_Wv"]).reshape(16, 128, 512))
    p["l1_wo"] = mslab(_np(inputs["L1_attn_Wo"]))

    gw = _np(inputs["L1_ffn_gate_w"]).astype(np.float32)
    gw_pad = np.zeros((2048, 128), np.float32)
    gw_pad[:, :8] = gw.T
    p["l1_gw"] = mslab(gw_pad)
    p["l1_sgw"] = mslab(np.ascontiguousarray(_np(inputs["L1_ffn_shared_gw"]).astype(np.float32).T))
    p["l1_sdw"] = mslab(np.ascontiguousarray(_np(inputs["L1_ffn_shared_dw"]).astype(np.float32).T))
    up = _np(inputs["L1_ffn_up_w"]).astype(np.float32)
    dn = _np(inputs["L1_ffn_down_w"]).astype(np.float32)
    p["l1_up"] = np.stack([mslab(np.ascontiguousarray(up[e].T)) for e in range(8)])
    p["l1_dn"] = np.stack([mslab(np.ascontiguousarray(dn[e].T)) for e in range(8)])

    cn = np.zeros((128, NCONST), np.float32)
    def put(name, v):
        o = CONST_OFF[name]
        v = np.asarray(v, np.float32)
        cn[:, o:o + v.shape[1]] = v
    put("l0_bq", pack_feat(inputs["L0_attn_bq"]))
    put("l0_bk", pack_feat(inputs["L0_attn_bk"]))
    put("l0_bo", pack_feat(inputs["L0_attn_bo"]))
    put("l0_sg", pack_feat(inputs["L0_ffn_s_gate"]))
    put("l0_su", pack_feat(inputs["L0_ffn_s_up"]))
    put("l0_sd", pack_feat(inputs["L0_ffn_s_down"]))
    put("l1_bq", pack_feat(_np(inputs["L1_attn_bq"]).astype(np.float32)))
    put("l1_bk", pack_feat(_np(inputs["L1_attn_bk"]).astype(np.float32)))
    put("gbias", np.broadcast_to(_np(inputs["L1_ffn_gate_bias"]).astype(np.float32), (128, 8)))
    cn[:, CONST_OFF["eps"]] = EPS
    put("l0_bvb", np.broadcast_to(_np(inputs["L0_attn_bv"]).astype(np.float32), (128, 512)))
    put("l1_bvb", np.broadcast_to(_np(inputs["L1_attn_bv"]).astype(np.float32), (128, 512)))
    p["consts"] = cn
    p["pos_off"] = [
        np.broadcast_to(np.array([[512.0 * CHUNKS[c % 2][0], 512.0 * CHUNKS[c % 2][1]]],
                                 np.float32), (128, 2)).copy()
        for c in range(8)]
    return p

SUB_WEIGHTS = {
    0: ["l0_wg", "l0_wu", "l0_wd"],
    1: ["l0_wq", "l0_wk", "l0_wvk", "l0_wo"],
    2: ["l0_wg", "l0_wu", "l0_wd"],
    3: ["l1_gw", "l1_sgw", "l1_sdw", "l1_up", "l1_dn"],
    4: ["l1_wq", "l1_wk", "l1_wvk", "l1_wo"],
    5: ["l1_gw", "l1_sgw", "l1_sdw", "l1_up", "l1_dn"],
}
W_DTYPE = {"l0_wq": F32, "l0_wk": F32, "l0_wvk": F32, "l0_wo": F32,
           "l0_wg": I8, "l0_wu": I8, "l0_wd": I8,
           "l1_wq": BF16, "l1_wk": BF16, "l1_wvk": BF16, "l1_wo": BF16,
           "l1_gw": F32, "l1_sgw": F32, "l1_sdw": F32, "l1_up": F32, "l1_dn": F32}
W_SHAPE = {"l0_wq": [16, 128, 2048], "l0_wk": [4, 128, 2048], "l0_wvk": [16, 128, 512],
           "l0_wo": [16, 128, 2048], "l0_wg": [64, 128, 2048], "l0_wu": [64, 128, 2048],
           "l0_wd": [16, 128, 8192],
           "l1_wq": [16, 128, 2048], "l1_wk": [4, 128, 2048], "l1_wvk": [16, 128, 512],
           "l1_wo": [16, 128, 2048],
           "l1_gw": [1, 128, 2048], "l1_sgw": [44, 128, 2048], "l1_sdw": [16, 128, 2816],
           "l1_up": [8, 22, 128, 2048], "l1_dn": [8, 16, 128, 1408]}


COMPUTE_STREAMS = {"pe": "tensor", "dve": "vector", "act": "scalar", "gpc": "gpsimd"}
DMA_CHANS = ["io", "kv", "w0", "w1", "w2", "w3", "w4", "w5"]

class Prog:
    def __init__(self):
        self.echo_fn = None
        self.ops = []
        self.sops = {s: [] for s in list(COMPUTE_STREAMS) + DMA_CHANS}
        self.bufs = {}

    def _buf(self, k):
        return self.bufs.setdefault(k, dict(writer=None, readers=[], pending=set()))

    def op(self, stream, fn, reads=(), writes=(), engine=None):
        if engine is None:
            engine = COMPUTE_STREAMS[stream]
        reads = list(dict.fromkeys(reads))
        writes = list(dict.fromkeys(writes))
        deps = set()
        for k in reads:
            b = self._buf(k)
            deps |= b["pending"]
            b["pending"] = set()
            if b["writer"] is not None:
                deps.add(b["writer"])
        for k in writes:
            b = self._buf(k)
            deps |= b["pending"]
            b["pending"] = set()
            if b["writer"] is not None:
                deps.add(b["writer"])
            deps.update(b["readers"])
        evt = (stream, len(self.sops[stream]))
        dmap = {}
        for (s, q) in deps:
            if s == stream and s == "pe":
                continue
            dmap[s] = max(dmap.get(s, -1), q)
        o = dict(stream=stream, fn=fn, deps=dmap, marked=False, engine=engine)
        self.ops.append(o)
        self.sops[stream].append(o)
        for k in reads:
            self._buf(k)["readers"].append(evt)
        for k in writes:
            b = self._buf(k)
            b["writer"] = evt
            b["readers"] = []
        return evt

    def dma(self, chan, engine, fn, reads=(), writes=(), echo=False):
        evt = self.op(chan, fn, reads=reads, writes=writes, engine=engine)
        if echo and self.echo_fn is not None:
            eevt = self.op(chan, self.echo_fn, engine=engine)
            for k in writes:
                self.bufs[k]["writer"] = eevt
        return evt

    def retag(self, new_keys, old_keys):
        pend = set()
        for k in old_keys:
            if k in self.bufs:
                b = self.bufs.pop(k)
                if b["writer"] is not None:
                    pend.add(b["writer"])
                pend.update(b["readers"])
                pend |= b["pending"]
        for k in new_keys:
            b = self._buf(k)
            b["pending"] |= pend

    def finalize_and_emit(self, nc, es):
        step = lambda s: 16 if s in DMA_CHANS else 1
        for o in self.ops:
            for s, q in o["deps"].items():
                self.sops[s][q]["marked"] = True
        for s in DMA_CHANS:
            for o in self.sops[s]:
                o["marked"] = True
        finals = {}
        for s, ops in self.sops.items():
            c = 0
            for o in ops:
                if o["marked"]:
                    c += 1
                o["count"] = c * step(s)
            finals[s] = c * step(s)
        sems = {s: es.enter_context(nc.semaphore(f"sem_{s}")) for s in self.sops}
        engines = ["tensor", "vector", "scalar", "gpsimd", "sync"]
        by_engine = {e: [] for e in engines}
        for o in self.ops:
            by_engine[o["engine"]].append(o)

        def runner(eng_obj, engine_name):
            waited = {}
            for o in by_engine[engine_name]:
                if o["stream"] in DMA_CHANS and o["count"] > 16:
                    v = o["count"] - 16
                    if v > waited.get(o["stream"], 0):
                        eng_obj.wait_ge(sems[o["stream"]], v)
                        waited[o["stream"]] = v
                for s in sorted(o["deps"]):
                    q = o["deps"][s]
                    dep = self.sops[s][q]
                    assert dep["marked"]
                    v = dep["count"]
                    if v > waited.get(s, 0):
                        eng_obj.wait_ge(sems[s], v)
                        waited[s] = v
                ins = o["fn"](eng_obj)
                if o["marked"]:
                    ins.then_inc(sems[o["stream"]], step(o["stream"]))
            for s in sorted(self.sops):
                if finals[s] > waited.get(s, 0):
                    eng_obj.wait_ge(sems[s], finals[s])

        with nc.Block() as blk:
            blk.sync(lambda e: runner(e, "sync"))
            blk.gpsimd(lambda e: runner(e, "gpsimd"))
            blk.vector(lambda e: runner(e, "vector"))
            blk.scalar(lambda e: runner(e, "scalar"))
            blk.tensor(lambda e: runner(e, "tensor"))

def mm_accum(P, bank_ap, bank_key, parts):
    n = len(parts)
    for i, (l, r, rk) in enumerate(parts):
        P.op("pe",
             (lambda pe, l=l, r=r, st=(i == 0), sp=(i == n - 1):
              pe.matmul(bank_ap, l, r, start=st, stop=sp)),
             reads=list(rk), writes=[bank_key])


class Ctx:
    pass

def atk(lo, hi):
    return [("at", i) for i in range(lo // 512, (hi + 511) // 512)]

def build_program(subs):
    nc = bass.Bass("TRN2", target_bir_lowering=False,
                   detect_race_conditions=DETECT_RACES)
    es = ExitStack()
    X = Ctx()
    X.in_names = []
    X.nc = nc
    X.es = es

    def din(name, shape, dt):
        X.in_names.append(name)
        return nc.dram_tensor(name, list(shape), dt, kind="ExternalInput").ap()

    x_in = din("x_in", [D, T], F32)
    consts_in = din("consts", [128, NCONST], F32)
    pos_in = din("pos_off", [128, 2], F32)
    w_in = {}
    for w in sorted({w for s in subs for w in SUB_WEIGHTS.get(s, [])}):
        w_in[w] = din(w, W_SHAPE[w], W_DTYPE[w])
    x_out = nc.dram_tensor("x_out", [D, T], F32, kind="ExternalOutput").ap()

    echo_dram = nc.dram_tensor("echo_dram", [1, 64], F32, kind="Internal").ap()

    kvc = kva = None
    if any(s in (1, 4) for s in subs):
        KF = 4 * 128 * T
        VF = 8 * 128 * 512
        SF = 128 * 4
        kvc = (nc.dram_tensor("kvcK", [1, KF + SF], BF16, kind="Internal").ap(),
               nc.dram_tensor("kvcV", [1, VF], BF16, kind="Internal").ap())
        kva = (nc.dram_tensor("kvaK", [2, KF + SF], BF16, kind="Internal").ap(),
               nc.dram_tensor("kvaV", [2, VF], BF16, kind="Internal").ap())

    A = {}
    def sbuf(name, shape, dt):
        A[name] = nc.alloc_sbuf_tensor(name, list(shape), dt)

    sbuf("ax", [128, KT * T], F32)
    sbuf("axn", [128, KT * T], BF16)
    sbuf("aq", [128, KT * T], BF16)
    sbuf("ak", [128, 4 * 2048], BF16)
    sbuf("av", [128, 16 * 512], BF16)
    sbuf("aw", [128, 8192], BF16)
    sbuf("awf", [128, 1024], F32)
    sbuf("at", [128, 2560], F32)
    sbuf("atb", [128, 1024], BF16)
    sbuf("amask", [128, 2560], BF16)
    sbuf("acs", [128, 2048], BF16)
    sbuf("asm", [128, 88], F32)
    sbuf("cn", [128, NCONST], F32)
    sbuf("pos", [128, 2], F32)
    sbuf("onesq", [128, 128], BF16)
    sbuf("onesqf", [128, 128], F32)
    sbuf("identf", [128, 128], F32)

    sbuf("echo_sb", [1, 16], F32)
    P_echo = lambda e: e.dma_start(A["echo_sb"][:, :], echo_dram)
    PS = [es.enter_context(nc.psum_tensor(f"pb{i}", [128, 512], F32)) for i in range(7)]
    X.PT = es.enter_context(nc.psum_tensor("pt", [128, 128], F32))

    P = Prog()
    P.echo_fn = P_echo
    X.P, X.A, X.PS = P, A, PS
    X.w_in = w_in

    X.xt = lambda kt: A["ax"][:, kt * T:(kt + 1) * T]
    X.xc = lambda kt, c: A["ax"][:, kt * T + c * 512: kt * T + (c + 1) * 512]
    X.xnt = lambda kt: A["axn"][:, kt * T:(kt + 1) * T]
    X.xnc = lambda kt, c: A["axn"][:, kt * T + c * 512: kt * T + (c + 1) * 512]
    X.cnv = lambda name, i: A["cn"][:, CONST_OFF[name] + i: CONST_OFF[name] + i + 1]
    X.wreg = lambda i: A["aw"][:, i * 2048:(i + 1) * 2048]

    nc.const_aps.register(F32, EPS, A["cn"][:, CONST_OFF["eps"]:CONST_OFF["eps"] + 1])
    X.bank_i = 0
    def next_bank():
        i = X.bank_i % 7
        X.bank_i += 1
        return PS[i], ("ps", i)
    X.next_bank = next_bank

    P.dma("io", "sync", lambda e: e.dma_start(A["cn"][:, :], consts_in), writes=[("cn",)], echo=False)
    P.dma("io", "sync", lambda e: e.dma_start(A["pos"][:, :], pos_in), writes=[("pos",)], echo=False)
    for c in range(C2):
        P.dma("io", "sync", lambda e, c=c: e.dma_start(
            A["ax"][:, :].rearrange("p (kt t) -> p kt t", kt=KT)[:, :, c * 512:(c + 1) * 512],
            x_in.rearrange("(kt p) t -> p kt t", p=128)[:, :, c * 512:(c + 1) * 512]),
            writes=[("x", c)], echo=False)
    P.op("dve", lambda e: e.memset(A["onesq"][:, :], 1.0), writes=[("onesq",)])
    P.op("dve", lambda e: e.memset(A["onesqf"][:, :], 1.0), writes=[("onesqf",)])
    P.op("gpc", lambda e: e.memset(A["identf"][:, :], 1.0), writes=[("identf",)])
    P.op("gpc", lambda e: e.affine_select(
        out=A["identf"][:, :], in_=A["identf"][:, :], compare_op=ALU.is_equal,
        fill=0.0, base=0, pattern=[[-1, 128]], channel_multiplier=1),
        reads=[("identf",)], writes=[("identf",)])
    if any(s in (0, 2) for s in subs):
        o = CONST_OFF["scratch"]
        sd = CONST_OFF["l0_sd"]
        P.op("dve", lambda e, o=o, sd=sd: e.tensor_single_scalar(
            A["cn"][:, o:o + 16], A["cn"][:, sd:sd + 16], 0.5, ALU.mult),
            reads=[("cn",)], writes=[("cn",)])
    X.SD2 = CONST_OFF["scratch"]

    for s in subs:
        if s == 99:
            emit_ln(X)
        elif s in (0, 2):
            emit_ffn0(X)
        elif s == 1:
            emit_attn(X, 0, kvc, kva)
        elif s == 4:
            emit_attn(X, 1, kvc, kva)
        else:
            emit_ffn1(X)

    P.dma("io", "sync", lambda e: e.dma_start(
        x_out.rearrange("(kt p) t -> p kt t", p=128),
        A["ax"][:, :].rearrange("p (kt t) -> p kt t", kt=KT)),
        reads=[("x", 0), ("x", 1)], echo=False)

    if DEBUG_TAPS:
        for nm, dt_, keys in [("axn", BF16, [("xn", kt) for kt in range(KT)] + [("e", 0), ("e", 1)]),
                              ("aq", BF16, [("fu", j) for j in range(KT)] +
                               [("qa", h, c) for h in range(KT) for c in range(C2)]),
                              ("at", F32, [("at", i) for i in range(6)]),
                              ("ak", BF16, [("kt",)]), ("av", BF16, [("vt",)])]:
            live = [k for k in keys if k in P.bufs]
            if not live:
                continue
            dbg = nc.dram_tensor("dbg_" + nm, [128, A[nm].shape[1]], dt_,
                                 kind="ExternalOutput").ap()
            P.dma("io", "gpsimd", lambda e, dbg=dbg, nm=nm: e.dma_start(dbg, A[nm][:, :]),
                 reads=live, echo=False)

    P.finalize_and_emit(nc, es)
    return nc, X.in_names


def emit_ln(X):
    P, A = X.P, X.A
    atb, at = A["atb"], A["at"]
    P.retag([("xn", kt, c) for kt in range(KT) for c in range(C2)],
            [("e", pp, pt) for pp in range(2) for pt in range(16)])
    s1 = [X.next_bank() for _ in range(C2)]
    s2 = [X.next_bank() for _ in range(C2)]
    for c in range(C2):
        sq = atb[:, c * 512:(c + 1) * 512]
        for kt in range(KT):
            P.op("dve", lambda e, kt=kt, c=c: e.tensor_copy(X.xnc(kt, c), X.xc(kt, c)),
                 reads=[("x", c)], writes=[("xn", kt, c)])
            P.op("act", lambda e, kt=kt, c=c, sq=sq: e.activation(
                sq, X.xnc(kt, c), AF.Square, bias=0.0, scale=1.0),
                reads=[("xn", kt, c)], writes=[("atb", c)])
            P.op("pe", lambda e, kt=kt, c=c: e.matmul(
                s1[c][0][:, :], A["onesq"][:, :], X.xnc(kt, c),
                start=(kt == 0), stop=(kt == KT - 1)),
                reads=[("xn", kt, c), ("onesq",)], writes=[s1[c][1]])
            P.op("pe", lambda e, kt=kt, c=c, sq=sq: e.matmul(
                s2[c][0][:, :], A["onesq"][:, :], sq,
                start=(kt == 0), stop=(kt == KT - 1)),
                reads=[("atb", c), ("onesq",)], writes=[s2[c][1]])
    for c in range(C2):
        mu = at[:, c * 512:(c + 1) * 512]
        r = at[:, 1024 + c * 512: 1024 + (c + 1) * 512]
        sc = at[:, 2048:2560]
        P.op("dve", lambda e, c=c, mu=mu: e.tensor_single_scalar(
            mu, s1[c][0][:, :], 1.0 / D, ALU.mult),
            reads=[s1[c][1]], writes=atk(c * 512, (c + 1) * 512))
        P.op("dve", lambda e, mu=mu, sc=sc: e.tensor_mul(sc, mu, mu),
             reads=atk(c * 512, (c + 1) * 512), writes=atk(2048, 2560))
        P.op("dve", lambda e, c=c, sc=sc: e.scalar_tensor_tensor(
            sc, s2[c][0][:, :], 1.0 / D, sc, ALU.mult, ALU.subtract),
            reads=[s2[c][1]] + atk(2048, 2560), writes=atk(2048, 2560))
        P.op("act", lambda e, sc=sc: e.activation(sc, sc, AF.Ln, bias=EPS, scale=1.0),
             reads=atk(2048, 2560) + [("cn",)], writes=atk(2048, 2560))
        P.op("act", lambda e, r=r, sc=sc: e.activation(r, sc, AF.Exp, bias=0.0, scale=-0.5),
             reads=atk(2048, 2560), writes=atk(1024 + c * 512, 1024 + (c + 1) * 512))
        tmp = at[:, 2048:2560]
        for kt in range(KT):
            P.op("dve", lambda e, kt=kt, c=c, mu=mu, tmp=tmp: e.tensor_sub(
                tmp, X.xc(kt, c), mu),
                reads=[("x", c)] + atk(c * 512, (c + 1) * 512), writes=atk(2048, 2560))
            P.op("dve", lambda e, kt=kt, c=c, r=r, tmp=tmp: e.tensor_mul(
                X.xnc(kt, c), tmp, r),
                reads=atk(2048, 2560) + atk(1024 + c * 512, 1024 + (c + 1) * 512),
                writes=[("xn", kt, c)])


def emit_ffn0(X):
    P, A = X.P, X.A
    emit_ln(X)
    P.retag([("fu", j) for j in range(KT)],
            [("qa", h, c) for h in range(KT) for c in range(C2)])
    aq, at = A["aq"], A["at"]
    wg, wu, wd = X.w_in["l0_wg"], X.w_in["l0_wu"], X.w_in["l0_wd"]
    NG, GT = 4, 16
    for g in range(NG):
        for j in range(GT):
            mt = g * GT + j
            bi = mt % 2
            wgv, kg = X.wreg(bi), ("w", bi)
            wuv, ku = X.wreg(2 + bi), ("w", 2 + bi)
            P.dma(f"w{bi}", "gpsimd", lambda e, mt=mt, wgv=wgv: e.dma_start(wgv, wg[mt]), writes=[kg])
            P.dma(f"w{2+bi}", "gpsimd", lambda e, mt=mt, wuv=wuv: e.dma_start(wuv, wu[mt]), writes=[ku])
            for c in range(C2):
                bg, bgk = X.next_bank()
                bu, buk = X.next_bank()
                mm_accum(P, bg[:, :], bgk,
                         [(wgv[:, kt * 128:(kt + 1) * 128], X.xnc(kt, c), [kg, ("xn", kt, c)])
                          for kt in range(KT)])
                mm_accum(P, bu[:, :], buk,
                         [(wuv[:, kt * 128:(kt + 1) * 128], X.xnc(kt, c), [ku, ("xn", kt, c)])
                          for kt in range(KT)])
                pp = (2 * j + c) % 2
                tg = at[:, pp * 1024: pp * 1024 + 512]
                tu = at[:, pp * 1024 + 512: pp * 1024 + 1024]
                P.op("act", lambda e, bg=bg, tg=tg, mt=mt: e.activation(
                    tg, bg[:, :], AF.Silu, bias=0.0, scale=X.cnv("l0_sg", mt)),
                    reads=[bgk, ("cn",)], writes=atk(pp * 1024, pp * 1024 + 512))
                P.op("dve", lambda e, bu=bu, tu=tu, mt=mt: e.tensor_single_scalar(
                    tu, bu[:, :], X.cnv("l0_su", mt), ALU.mult),
                    reads=[buk, ("cn",)], writes=atk(pp * 1024 + 512, pp * 1024 + 1024))
                P.op("dve", lambda e, tg=tg, tu=tu, j=j, c=c: e.tensor_mul(
                    aq[:, j * T + c * 512: j * T + (c + 1) * 512], tg, tu),
                    reads=atk(pp * 1024, pp * 1024 + 1024), writes=[("fu", j)])
        for mo in range(KT):
            bi = mo % 2
            wdv, kd = X.wreg(bi), ("w", bi)
            P.dma(f"w{bi}", "gpsimd", lambda e, mo=mo, g=g, wdv=wdv: e.dma_start(
                wdv, wd[mo][:, g * 2048:(g + 1) * 2048]), writes=[kd])
            for c in range(C2):
                bd, bdk = X.next_bank()
                mm_accum(P, bd[:, :], bdk,
                         [(wdv[:, jj * 128:(jj + 1) * 128],
                           aq[:, jj * T + c * 512: jj * T + (c + 1) * 512],
                           [kd, ("fu", jj)]) for jj in range(GT)])
                if DEBUG_TAPS and g == 3 and mo < 3 and c == 0:
                    P.op("dve", lambda e, bd=bd, mo=mo: e.tensor_copy(
                        A["av"][:, mo * 512:(mo + 1) * 512], bd[:, :]),
                        reads=[bdk], writes=[("vt",)])
                    P.op("dve", lambda e, wdv=wdv, mo=mo: e.tensor_copy(
                        A["ak"][:, mo * 2048:(mo + 1) * 2048], wdv),
                        reads=[kd], writes=[("kt",)])
                P.op("dve", lambda e, bd=bd, mo=mo, c=c: e.scalar_tensor_tensor(
                    X.xc(mo, c), bd[:, :], A["cn"][:, X.SD2 + mo: X.SD2 + mo + 1],
                    X.xc(mo, c), ALU.mult, ALU.add),
                    reads=[bdk, ("cn",), ("x", c)], writes=[("x", c)])
                if DEBUG_TAPS and mo == 0 and c == 0:
                    P.op("dve", lambda e, g=g: e.tensor_copy(
                        A["av"][:, (4 + g) * 512:(5 + g) * 512], X.xc(0, 0)),
                        reads=[("x", 0)], writes=[("vt",)])


def emit_attn(X, layer, kvc, kva):
    P, A = X.P, X.A
    aw, at, atb, aq, ak, av, asm = (A["aw"], A["at"], A["atb"], A["aq"],
                                    A["ak"], A["av"], A["asm"])
    amask = A["amask"]
    pfx = "l1_" if layer else "l0_"
    wq, wk, wvk, wo = (X.w_in[pfx + "wq"], X.w_in[pfx + "wk"],
                       X.w_in[pfx + "wvk"], X.w_in[pfx + "wo"])
    weng = "gpsimd"

    if layer:
        emit_cossin(X)
    emit_ln(X)
    P.retag([("qa", h, c) for h in range(KT) for c in range(C2)],
            [("fu", j) for j in range(KT)])
    P.retag([("ksq", 0), ("ksq", 1)], [("w", 2)])

    bvb = A["cn"][:, CONST_OFF[pfx + "bvb"]:CONST_OFF[pfx + "bvb"] + 512]
    KS = lambda h2: aw[:, 4096 + h2 * 1024: 4096 + (h2 + 1) * 1024]

    def kstat(mk):
        h2 = mk % 2
        ks = KS(h2)
        sA = asm[:, 8 + 2 * h2: 9 + 2 * h2]
        sB = asm[:, 9 + 2 * h2: 10 + 2 * h2]
        bs = [X.next_bank() for _ in range(C2)]
        for c in range(C2):
            P.op("pe", lambda e, b=bs[c][0], ks=ks, c=c: e.matmul(
                b[:, :], A["onesq"][:, :], ks[:, c * 512:(c + 1) * 512],
                start=True, stop=True), reads=[("ksq", h2), ("onesq",)], writes=[bs[c][1]])
        P.op("dve", lambda e, sA=sA: e.reduce_max(sA, bs[0][0][:, :], axis=AX),
             reads=[bs[0][1]], writes=[("k2m", 2 * h2)])
        P.op("dve", lambda e, sB=sB: e.reduce_max(sB, bs[1][0][:, :], axis=AX),
             reads=[bs[1][1]], writes=[("k2m", 2 * h2 + 1)])
        P.op("dve", lambda e, mk=mk, sA=sA, sB=sB: e.tensor_max(
            asm[:, mk:mk + 1], sA, sB),
            reads=[("k2m", 2 * h2), ("k2m", 2 * h2 + 1)], writes=[("mk2own", mk)])

    pend = None
    for mk in range(4):
        bi = mk % 2
        wv_, kw = X.wreg(bi), ("w", bi)
        P.dma(f"w{bi}", weng, lambda e, mk=mk, wv_=wv_: e.dma_start(wv_, wk[mk]), writes=[kw])
        for c in range(C2):
            b, bk = X.next_bank()
            mm_accum(P, b[:, :], bk,
                     [(wv_[:, kt * 128:(kt + 1) * 128], X.xnc(kt, c), [kw, ("xn", kt, c)])
                      for kt in range(KT)])
            P.op("dve", lambda e, b=b, mk=mk, c=c: e.tensor_single_scalar(
                ak[:, mk * T + c * 512: mk * T + (c + 1) * 512], b[:, :],
                X.cnv(pfx + "bk", mk), ALU.add),
                reads=[bk, ("cn",)], writes=[("kt",)])
        P.op("dve", lambda e, mk=mk: e.tensor_mul(
            KS(mk % 2), ak[:, mk * T:(mk + 1) * T], ak[:, mk * T:(mk + 1) * T]),
            reads=[("kt",)], writes=[("ksq", mk % 2)])
        if pend is not None:
            pend()
        pend = (lambda mk=mk: kstat(mk))
        if layer:
            emit_rope(X, ak[:, mk * T:(mk + 1) * T], [("kt",)])
    pend()
    P.op("dve", lambda e: e.tensor_copy(amask[:, SOFF:SOFF + 4], asm[:, 0:4]),
         reads=[("mk2own", mk) for mk in range(4)], writes=[("mk2b",)])
    KF = 4 * 128 * T
    VF = 8 * 128 * 512
    kvcK, kvcV = kvc
    kvaK, kvaV = kva
    P.dma("kv", "sync", lambda e: e.dma_start(
        kvcK[0:1, 0:KF].rearrange("r (kv p t) -> (r p) kv t", kv=4, p=128),
        ak[:, 0:4096].rearrange("p (kv t) -> p kv t", kv=4)),
        reads=[("kt",)], writes=[("kvcK",)])
    P.dma("kv", "sync", lambda e: e.dma_start(
        kvcK[0:1, KF:].rearrange("r (p c) -> (r p) c", p=128),
        amask[:, SOFF:SOFF + 4]),
        reads=[("mk2b",)], writes=[("kvcK",)])
    P.dma("w0", weng, lambda e: e.dma_start(
        aw[:, 0:8192].rearrange("p (kt d) -> p kt d", kt=16),
        wvk.rearrange("kt p d -> p kt d")),
        writes=[("w", 0), ("w", 1), ("w", 2), ("w", 3), ("ksq", 0), ("ksq", 1)])
    P.op("gpc", lambda e: e.collective_compute(
        "AllGather", ALU.bypass, replica_groups=RG_PAIRS,
        ins=[kvcK.opt()], outs=[kvaK.opt()]),
        reads=[("kvcK",)], writes=[("kvaK",)])
    for ptl in range(8):
        b, bk = X.next_bank()
        mm_accum(P, b[:, :], bk,
                 [(X.xnt(kt)[:, ptl * 128:(ptl + 1) * 128], aw[:, kt * 512:(kt + 1) * 512],
                   [("xn", kt, ptl // 4), ("w", kt // 4)] +
                   ([("ksq", 0)] if kt in (8, 9) else [("ksq", 1)] if kt in (10, 11) else []))
                  for kt in range(KT)])
        P.op("dve", lambda e, b=b, ptl=ptl: e.tensor_add(
            av[:, ptl * 512:(ptl + 1) * 512], b[:, :], bvb),
            reads=[bk, ("cn",)], writes=[("vt",)])
    P.dma("kv", "sync", lambda e: e.dma_start(
        kvcV[0:1, :].rearrange("r (vt p d) -> (r p) vt d", vt=8, p=128),
        av[:, 0:4096].rearrange("p (vt d) -> p vt d", vt=8)),
        reads=[("vt",)], writes=[("kvcV",)])

    def emit_vcoll_readback():
        P.op("gpc", lambda e: e.collective_compute(
            "AllGather", ALU.bypass, replica_groups=RG_PAIRS,
            ins=[kvcV.opt()], outs=[kvaV.opt()]),
            reads=[("kvcV",)], writes=[("kvaV",)])
        for r in range(2):
            P.dma("kv", "sync", lambda e, r=r: e.dma_start(
                amask[:, SOFF + 4 + 4 * r: SOFF + 8 + 4 * r],
                kvaK[r:r + 1, KF:].rearrange("r (p c) -> (r p) c", p=128)),
                reads=[("kvaK",)], writes=[("mk2r", r)])
        for r in range(2):
            for kv4 in range(4):
                P.dma("kv", "sync", lambda e, r=r, kv4=kv4: e.dma_start(
                    ak[:, kv4 * 2048 + r * T: kv4 * 2048 + (r + 1) * T],
                    kvaK[r:r + 1, kv4 * 128 * T:(kv4 + 1) * 128 * T].rearrange(
                        "r (p t) -> (r p) t", p=128)),
                    reads=[("kvaK",)], writes=[("kt",)])
        for r in range(2):
            P.dma("kv", "sync", lambda e, r=r: e.dma_start(
                av[:, r * 4096:(r + 1) * 4096].rearrange("p (vt d) -> p vt d", vt=8),
                kvaV[r:r + 1, :].rearrange("r (vt p d) -> (r p) vt d", vt=8, p=128)),
                reads=[("kvaV",)], writes=[("vt",)])

    def qstat(mq):
        h2 = mq % 2
        qs = KS(h2)
        for qc in range(C2):
            b, bk = X.next_bank()
            P.op("pe", lambda e, b=b, qs=qs, qc=qc: e.matmul(
                b[:, :], A["onesq"][:, :], qs[:, qc * 512:(qc + 1) * 512],
                start=True, stop=True), reads=[("ksq", h2), ("onesq",)], writes=[bk])
            P.op("dve", lambda e, b=b, mq=mq, qc=qc: e.reduce_max(
                asm[:, 12 + mq * 2 + qc: 13 + mq * 2 + qc], b[:, :], axis=AX),
                reads=[bk], writes=[("nb", mq, qc)])

    pend = None
    for mq in range(KT):
        bi = mq % 2
        wv_, kw = X.wreg(bi), ("w", bi)
        P.dma(f"w{bi}", weng, lambda e, mq=mq, wv_=wv_: e.dma_start(wv_, wq[mq]), writes=[kw])
        for c in range(C2):
            b, bk = X.next_bank()
            mm_accum(P, b[:, :], bk,
                     [(wv_[:, kt * 128:(kt + 1) * 128], X.xnc(kt, c), [kw, ("xn", kt, c)])
                      for kt in range(KT)])
            P.op("dve", lambda e, b=b, mq=mq, c=c: e.tensor_single_scalar(
                aq[:, mq * T + c * 512: mq * T + (c + 1) * 512], b[:, :],
                X.cnv(pfx + "bq", mq), ALU.add),
                reads=[bk, ("cn",)], writes=[("qa", mq, c)])
        P.op("dve", lambda e, mq=mq: e.tensor_mul(
            KS(mq % 2), aq[:, mq * T:(mq + 1) * T], aq[:, mq * T:(mq + 1) * T]),
            reads=[("qa", mq, 0), ("qa", mq, 1)], writes=[("ksq", mq % 2)])
        if pend is not None:
            pend()
        pend = (lambda mq=mq: qstat(mq))
        if layer:
            emit_rope(X, aq[:, mq * T:(mq + 1) * T], [("qa", mq, 0), ("qa", mq, 1)])
        if mq == 1:
            emit_vcoll_readback()
    pend()
    P.retag([("w", 2)], [("ksq", 0), ("ksq", 1)])
    P.op("dve", lambda e: e.tensor_max(
        amask[:, SOFF + 4:SOFF + 8], amask[:, SOFF + 4:SOFF + 8],
        amask[:, SOFF + 8:SOFF + 12]),
        reads=[("mk2r", 0), ("mk2r", 1)], writes=[("mk2r", 0)])
    P.op("dve", lambda e: e.tensor_copy(asm[:, 0:4], amask[:, SOFF + 4:SOFF + 8]),
         reads=[("mk2r", 0)], writes=[("mk2f",)])
    negB = asm[:, 12:44]
    for kv in range(4):
        blk = asm[:, 12 + kv * 8: 20 + kv * 8]
        P.op("dve", lambda e, blk=blk, kv=kv: e.tensor_single_scalar(
            blk, blk, asm[:, kv:kv + 1], ALU.mult),
            reads=[("nb", kv * 4 + g, qc) for g in range(4) for qc in range(C2)]
            + [("mk2f",)], writes=[("nbk", kv)])
    P.op("act", lambda e: e.activation(negB, negB, AF.Sqrt, bias=0.0, scale=1.0),
         reads=[("nbk", kv) for kv in range(4)], writes=[("negb",)])
    P.op("dve", lambda e: e.tensor_single_scalar(negB, negB, -SCALE, ALU.mult),
         reads=[("negb",)], writes=[("negb",)])
    P.retag([("e", pp, pt) for pp in range(2) for pt in range(16)],
            [("xn", kt, c) for kt in range(KT) for c in range(C2)])
    SC = (0, 1, 2)
    evt = lambda pp, pt: A["axn"][:, pp * 8192 + pt * 512: pp * 8192 + (pt + 1) * 512]

    def fin(st):
        h, qc, pp, rsb, pvb, kv, s = st[:7]
        par = s % 2
        rec = at[:, 2048:2560] if par == 0 else at[:, 0:512]
        rk = atk(2048, 2560) if par == 0 else atk(0, 512)
        P.op("dve", lambda e, rec=rec, rsb=rsb: e.tensor_single_scalar(
            rec, X.PS[rsb][:, :], 1e-30, ALU.add), reads=[("ps", rsb)], writes=rk)
        P.op("dve", lambda e, rec=rec: e.reciprocal(rec, rec), reads=rk, writes=rk)
        P.op("dve", lambda e, rec=rec, pvb=pvb, h=h, qc=qc: e.tensor_mul(
            aq[:, h * T + qc * 512: h * T + (qc + 1) * 512], X.PS[pvb][:, :], rec),
            reads=[("ps", pvb)] + rk, writes=[("qa", h, qc)])

    def rs_pv_part(st, idx):
        ph, pqc, ppp, prsb, ppvb, pkv, ps_, ptiles = st
        if idx >= len(ptiles):
            return
        pt = ptiles[idx]
        last = idx == len(ptiles) - 1
        P.op("pe", lambda e, prsb=prsb, ppp=ppp, pt=pt, f=(idx == 0), l=last: e.matmul(
            X.PS[prsb][:, :], A["onesq"][:, :], evt(ppp, pt),
            start=f, stop=l),
            reads=[("onesq",), ("e", ppp, pt)], writes=[("ps", prsb)])
        P.op("pe", lambda e, ppvb=ppvb, ppp=ppp, pkv=pkv, pt=pt, f=(idx == 0), l=last: e.matmul(
            X.PS[ppvb][:, :],
            av[:, pt * 512 + pkv * 128: pt * 512 + (pkv + 1) * 128],
            evt(ppp, pt),
            start=f, stop=l),
            reads=[("vt",), ("e", ppp, pt)], writes=[("ps", ppvb)])

    prev = None
    s = 0
    for qc in range(C2):
        tiles = LIVE_T[qc] if layer else list(range(16))
        masked = MASK_T[qc] if layer else ()
        if layer:
            emit_mask(X, qc)
        for kv in range(4):
            for g in range(4):
                h = kv * 4 + g
                pp = s % 2
                rsb, pvb = 3 + (s % 2), 5 + (s % 2)
                for idx, pt in enumerate(tiles):
                    sb = SC[idx % 3]
                    P.op("pe", lambda e, sb=sb, kv=kv, pt=pt, h=h, qc=qc: e.matmul(
                        X.PS[sb][:, :],
                        ak[:, kv * 2048 + pt * 128: kv * 2048 + (pt + 1) * 128],
                        aq[:, h * T + qc * 512: h * T + (qc + 1) * 512],
                        start=True, stop=True),
                        reads=[("kt",), ("qa", h, qc)], writes=[("ps", sb)])
                    P.op("act", lambda e, sb=sb, pp=pp, pt=pt, h=h, qc=qc: e.activation(
                        evt(pp, pt), X.PS[sb][:, :], AF.Exp,
                        bias=negB[:, h * 2 + qc: h * 2 + qc + 1], scale=SCALE),
                        reads=[("ps", sb), ("negb",)], writes=[("e", pp, pt)])
                    if pt in masked:
                        P.op("dve", lambda e, pp=pp, pt=pt: e.tensor_mul(
                            evt(pp, pt), evt(pp, pt),
                            amask[:, (pt // 8) * 896 + 384 - (pt % 4) * 128:
                                  (pt // 8) * 896 + 896 - (pt % 4) * 128]),
                            reads=[("e", pp, pt), ("mask",)], writes=[("e", pp, pt)])
                    if prev is not None:
                        rs_pv_part(prev, idx)
                if prev is not None:
                    for idx2 in range(len(tiles), len(prev[7])):
                        rs_pv_part(prev, idx2)
                    fin(prev)
                prev = (h, qc, pp, rsb, pvb, kv, s, tiles)
                s += 1
    for idx in range(len(prev[7])):
        rs_pv_part(prev, idx)
    fin(prev)
    for mo in range(KT):
        bi = mo % 2
        wv_, kw = X.wreg(bi), ("w", bi)
        P.dma(f"w{bi}", weng, lambda e, mo=mo, wv_=wv_: e.dma_start(wv_, wo[mo]), writes=[kw])
        for c in range(C2):
            b, bk = X.next_bank()
            mm_accum(P, b[:, :], bk,
                     [(wv_[:, kt * 128:(kt + 1) * 128],
                       aq[:, kt * T + c * 512: kt * T + (c + 1) * 512],
                       [kw, ("qa", kt, c)]) for kt in range(KT)])
            if layer == 0:
                P.op("dve", lambda e, b=b, mo=mo, c=c: e.scalar_tensor_tensor(
                    X.xc(mo, c), b[:, :], X.cnv("l0_bo", mo), X.xc(mo, c),
                    ALU.add, ALU.add),
                    reads=[bk, ("cn",), ("x", c)], writes=[("x", c)])
            else:
                P.op("dve", lambda e, b=b, mo=mo, c=c: e.tensor_add(
                    X.xc(mo, c), X.xc(mo, c), b[:, :]),
                    reads=[bk, ("x", c)], writes=[("x", c)])

def emit_rope(X, tile, keys):
    P, A = X.P, X.A
    atb, acs = A["atb"], A["acs"]
    rot = atb[:, 0:1024]
    cosb = acs[:, 0:1024]
    sinb = acs[:, 1024:2048]
    P.dma("io", "scalar", lambda e: e.dma_start(rot[0:64, :], tile[64:128, :]),
          reads=list(keys), writes=[("atb", 0), ("atb", 1)])
    P.dma("io", "scalar", lambda e: e.dma_start(rot[64:128, :], tile[0:64, :]),
          reads=list(keys), writes=[("atb", 0), ("atb", 1)])
    P.op("dve", lambda e: e.tensor_single_scalar(rot[0:64, :], rot[0:64, :], -1.0, ALU.mult),
         reads=[("atb", 0), ("atb", 1)], writes=[("atb", 0), ("atb", 1)])
    P.op("dve", lambda e: e.tensor_mul(tile, tile, cosb),
         reads=list(keys) + [("cs",)], writes=list(keys))
    P.op("dve", lambda e: e.tensor_mul(rot, rot, sinb),
         reads=[("atb", 0), ("atb", 1), ("cs",)], writes=[("atb", 0), ("atb", 1)])
    P.op("dve", lambda e: e.tensor_add(tile, tile, rot),
         reads=list(keys) + [("atb", 0), ("atb", 1)], writes=list(keys))

def emit_cossin(X):
    P, A = X.P, X.A
    at = A["at"]
    tg = at[:, 0:1024]
    sc = at[:, 1024:2048]
    sci = sc.bitcast(I32)
    iv = A["asm"][:, 48:49]
    P.op("gpc", lambda e: e.iota(iv, [[0, 1]], channel_multiplier=1,
                                 allow_small_or_imprecise_dtypes=True), writes=[("iv",)])
    iv2 = A["asm"][:, 52:53]
    P.op("dve", lambda e, iv2=iv2: e.tensor_single_scalar(iv2, iv, 64.0, ALU.is_ge),
         reads=[("iv",)], writes=[("iv2",)])
    P.op("dve", lambda e, iv2=iv2: e.scalar_tensor_tensor(iv, iv2, -64.0, iv, ALU.mult, ALU.add),
         reads=[("iv",), ("iv2",)], writes=[("iv",)])
    P.op("act", lambda e: e.activation(iv, iv, AF.Exp, bias=0.0,
                                       scale=float(-np.log(1e6) / 64.0)),
         reads=[("iv",)], writes=[("iv",)])
    P.op("gpc", lambda e: e.iota(tg, [[1, 1024]], channel_multiplier=0,
                                 allow_small_or_imprecise_dtypes=True),
         writes=atk(0, 1024))
    P.op("dve", lambda e: e.tensor_single_scalar(
        tg[:, 0:512], tg[:, 0:512], A["pos"][:, 0:1], ALU.add),
        reads=atk(0, 1024) + [("pos",)], writes=atk(0, 1024))
    P.op("dve", lambda e: e.tensor_single_scalar(
        tg[:, 512:1024], tg[:, 512:1024], A["pos"][:, 1:2], ALU.add),
        reads=atk(0, 1024) + [("pos",)], writes=atk(0, 1024))
    P.op("dve", lambda e: e.tensor_single_scalar(
        tg[:, 512:1024], tg[:, 512:1024], -512.0, ALU.add),
        reads=atk(0, 1024), writes=atk(0, 1024))
    P.op("dve", lambda e: e.tensor_single_scalar(tg, tg, iv, ALU.mult),
         reads=atk(0, 1024) + [("iv",)], writes=atk(0, 1024))
    for off, shift in ((1024, 0.0), (0, float(np.pi / 2))):
        P.op("dve", lambda e: e.tensor_single_scalar(
            sc, tg, float(1.0 / (2 * np.pi)), ALU.mult),
            reads=atk(0, 1024), writes=atk(1024, 2048))
        if shift:
            P.op("dve", lambda e, shift=shift: e.tensor_single_scalar(
                sc, sc, float(shift / (2 * np.pi)), ALU.add),
                reads=atk(1024, 2048), writes=atk(1024, 2048))
        P.op("dve", lambda e: e.tensor_copy(sci, sc),
             reads=atk(1024, 2048), writes=atk(1024, 2048))
        P.op("dve", lambda e: e.tensor_copy(sc, sci),
             reads=atk(1024, 2048), writes=atk(1024, 2048))
        P.op("dve", lambda e: e.scalar_tensor_tensor(
            sc, sc, float(-2 * np.pi), tg, ALU.mult, ALU.add),
            reads=atk(0, 2048), writes=atk(1024, 2048))
        if shift:
            P.op("dve", lambda e, shift=shift: e.tensor_single_scalar(
                sc, sc, float(shift), ALU.add),
                reads=atk(1024, 2048), writes=atk(1024, 2048))
        P.op("act", lambda e, off=off: e.activation(
            A["acs"][:, off:off + 1024], sc, AF.Sin, bias=0.0, scale=1.0),
            reads=atk(1024, 2048), writes=[("cs",)])

def emit_mask(X, qc):
    P, A = X.P, X.A
    P.retag([("mask",)], [("sel",), ("wet",)])
    P.op("gpc", lambda e: e.iota(A["at"][:, 0:896], [[1, 896]], channel_multiplier=-1,
                                 allow_small_or_imprecise_dtypes=True),
         writes=atk(0, 896))
    strips = sorted({pt // 4 for pt in MASK_T[qc]})
    for kc in strips:
        th = A["asm"][:, 84 + kc: 85 + kc]
        P.op("dve", lambda e, qc=qc, kc=kc, th=th: e.tensor_scalar(
            th, A["pos"][:, qc:qc + 1], -1.0, float(384 + KCB[kc]), ALU.mult, ALU.add),
            reads=[("pos",)], writes=[("mth", kc)])
        sl = kc >> 1
        P.op("dve", lambda e, sl=sl, th=th: e.tensor_single_scalar(
            A["amask"][:, sl * 896:(sl + 1) * 896], A["at"][:, 0:896], th, ALU.is_ge),
            reads=atk(0, 896) + [("mth", kc), ("mk2r", 0)], writes=[("mask",)])


def emit_ffn1(X):
    P, A = X.P, X.A
    aw, awf, at, aq, acs, asm = (A["aw"], A["awf"], A["at"], A["aq"],
                                 A["acs"], A["asm"])
    emit_ln(X)
    P.retag([("fu", j) for j in range(KT)],
            [("qa", h, c) for h in range(KT) for c in range(C2)])
    gwp = X.w_in["l1_gw"]
    akf = A["ak"][:, 0:4096].bitcast(F32)
    P.dma("w4", "gpsimd", lambda e, akf=akf: e.dma_start(akf, gwp[0]),
          writes=[("kt",)])
    bcs, bcsk = X.next_bank()
    bz = [X.next_bank() for _ in range(C2)]
    for kt in range(KT):
        P.op("pe", lambda e, kt=kt, st=(kt == 0), sp_=(kt == KT - 1): e.matmul(
            bcs[:, 0:128], A["onesqf"][:, :], akf[:, kt * 128:(kt + 1) * 128],
            start=st, stop=sp_), reads=[("onesqf",), ("kt",)], writes=[bcsk])
        for c in range(C2):
            P.op("pe", lambda e, kt=kt, c=c,
                 st=(kt == 0), sp_=(kt == KT - 1): e.matmul(
                bz[c][0][:, :], akf[:, kt * 128:(kt + 1) * 128], X.xc(kt, c),
                start=st, stop=sp_), reads=[("kt",), ("x", c)], writes=[bz[c][1]])
    cssb = at[:, 2048:2176]
    P.op("dve", lambda e, bcs=bcs, cssb=cssb: e.tensor_copy(cssb, bcs[:, 0:128]),
         reads=[bcsk], writes=atk(2048, 2176))
    P.op("pe", lambda e, cssb=cssb: e.transpose(X.PT[:, :], cssb, A["identf"][:, :]),
         reads=atk(2048, 2176) + [("identf",)], writes=[("pt",)])
    csc = asm[:, 50:51]
    P.op("dve", lambda e: e.tensor_copy(csc, X.PT[:, 0:1]), reads=[("pt",)], writes=[("csc",)])
    for c in range(C2):
        b, bk = bz[c]
        z = awf[:, c * 512:(c + 1) * 512]
        mu = at[:, c * 512:(c + 1) * 512]
        r = at[:, 1024 + c * 512: 1024 + (c + 1) * 512]
        t1 = at[:, 2048:2560]
        P.op("dve", lambda e, b=b, z=z: e.tensor_copy(z, b[:, :]),
             reads=[bk], writes=[("wf",)])
        P.op("dve", lambda e, mu=mu, t1=t1: e.tensor_single_scalar(t1, mu, csc, ALU.mult),
             reads=atk(c * 512, (c + 1) * 512) + [("csc",)], writes=atk(2048, 2560))
        P.op("dve", lambda e, z=z, t1=t1: e.tensor_sub(z, z, t1),
             reads=[("wf",)] + atk(2048, 2560), writes=[("wf",)])
        P.op("dve", lambda e, z=z, r=r: e.tensor_mul(z, z, r),
             reads=[("wf",)] + atk(1024 + c * 512, 1024 + (c + 1) * 512), writes=[("wf",)])
        P.op("act", lambda e, z=z: e.activation(z, z, AF.Sigmoid, bias=0.0, scale=1.0),
             reads=[("wf",)], writes=[("wf",)])
        for tt in range(4):
            g_tt = c * 4 + tt
            P.op("dve", lambda e, z=z, tt=tt: e.tensor_copy(
                A["atb"][:, 0:256].bitcast(F32)[:, :], z[:, tt * 128:(tt + 1) * 128]),
                reads=[("wf",)], writes=[("wtm",), ("wf",)])
            P.op("pe", lambda e: e.transpose(X.PT[:, :], A["atb"][:, 0:256].bitcast(F32)[:, :], A["identf"][:, :]),
                 reads=[("wtm",), ("identf",)], writes=[("pt",)])
            P.op("dve", lambda e, g_tt=g_tt: e.tensor_copy(
                A["atb"][:, 256:384].bitcast(F32)[:, g_tt * 8:(g_tt + 1) * 8], X.PT[:, 0:8]),
                reads=[("pt",)], writes=[("sct", g_tt), ("wf",)])
    def emit_topk():
        emit_sel8(X)
        emit_top2(X)

    gb = A["cn"][:, CONST_OFF["gbias"]:CONST_OFF["gbias"] + 8]
    emit_ffn1_shared_and_experts(X, emit_topk)


def emit_top2(X):
    P, A, asm = X.P, X.A, X.A["asm"]
    gb = A["cn"][:, CONST_OFF["gbias"]:CONST_OFF["gbias"] + 8]
    for tt in range(8):
        sc = A["atb"][:, 256:384].bitcast(F32)[:, tt * 8:(tt + 1) * 8]
        w1 = asm[:, 56:64]
        m1 = asm[:, 64:65]
        m2 = asm[:, 65:66]
        msk = asm[:, 66:74]
        msk2 = asm[:, 74:82]
        wsum = asm[:, 82:83]
        P.op("dve", lambda e, sc=sc: e.tensor_add(w1, sc, gb),
             reads=[("sct", tt), ("cn",)], writes=[("tk1",)])
        P.op("dve", lambda e: e.reduce_max(m1, w1, axis=AX),
             reads=[("tk1",)], writes=[("tkm1",)])
        P.op("dve", lambda e: e.tensor_single_scalar(msk, w1, m1, ALU.is_equal),
             reads=[("tk1",), ("tkm1",)], writes=[("tkk1",)])
        P.op("dve", lambda e: e.scalar_tensor_tensor(w1, msk, -1e30, w1, ALU.mult, ALU.add),
             reads=[("tkk1",), ("tk1",)], writes=[("tk1",)])
        P.op("dve", lambda e: e.reduce_max(m2, w1, axis=AX),
             reads=[("tk1",)], writes=[("tkm2",)])
        P.op("dve", lambda e: e.tensor_single_scalar(msk2, w1, m2, ALU.is_equal),
             reads=[("tk1",), ("tkm2",)], writes=[("tkk2",)])
        P.op("dve", lambda e: e.tensor_add(msk, msk, msk2),
             reads=[("tkk1",), ("tkk2",)], writes=[("tkk1",)])
        P.op("dve", lambda e, sc=sc: e.tensor_mul(msk, msk, sc),
             reads=[("tkk1",), ("sct", tt)], writes=[("tkk1",)])
        P.op("dve", lambda e: e.reduce_sum(wsum, msk, axis=AX),
             reads=[("tkk1",)], writes=[("tks",)])
        P.op("dve", lambda e: e.reciprocal(wsum, wsum), reads=[("tks",)], writes=[("tks",)])
        P.op("dve", lambda e: e.tensor_single_scalar(wsum, wsum, 0.5, ALU.mult),
             reads=[("tks",)], writes=[("tks",)])
        P.op("dve", lambda e: e.tensor_single_scalar(
            A["atb"][:, 0:256].bitcast(F32)[:, 0:8], msk, wsum, ALU.mult),
            reads=[("tkk1",), ("tks",)], writes=[("wtm",), ("wf",)])
        P.op("pe", lambda e: e.transpose(X.PT[:, :], A["atb"][:, 0:256].bitcast(F32)[:, :], A["identf"][:, :]),
             reads=[("wtm",), ("identf",)], writes=[("pt",)])
        P.op("dve", lambda e, tt=tt: e.tensor_copy(
            A["amask"][:, 1024:2048][:, tt * 128:(tt + 1) * 128], X.PT[:, :]),
            reads=[("pt",)], writes=[("wet",)])


def emit_ffn1_shared_and_experts(X, emit_topk):
    P, A, asm = X.P, X.A, X.A["asm"]
    aq, acs = A["aq"], A["acs"]
    sgw, sdw = X.w_in["l1_sgw"], X.w_in["l1_sdw"]
    for grp in range(2):
        emit_glu_group(
            X, n_j=11,
            g_slab=lambda j, grp=grp: sgw[22 + grp * 11 + j],
            u_slab=lambda j, grp=grp: sgw[grp * 11 + j],
            d_slab=lambda mo, grp=grp: (sdw[mo], grp * 11 * 128, 11 * 128),
            webk=None)
        if grp == 0:
            emit_topk()
    upw, dnw = X.w_in["l1_up"], X.w_in["l1_dn"]
    webc = acs[:, 0:1024]
    for ex in range(8):
        for c in range(C2):
            bwb, bwbk = X.next_bank()
            P.op("pe", lambda e, bwb=bwb, ex=ex, c=c: e.matmul(
                bwb[:, :], A["amask"][:, 0:1024].rearrange("p (a b) -> p a b", a=1)[:, 0, :] if False else A["amask"][:, 0:1024][:, ex * 128:(ex + 1) * 128],
                A["amask"][:, 1024:2048][:, c * 512:(c + 1) * 512], start=True, stop=True),
                reads=[("sel",), ("wet",)], writes=[bwbk])
            P.op("dve", lambda e, bwb=bwb, c=c: e.tensor_copy(
                webc[:, c * 512:(c + 1) * 512], bwb[:, :]),
                reads=[bwbk], writes=[("cs",)])
        emit_glu_group(
            X, n_j=11,
            g_slab=lambda j, ex=ex: upw[ex][j],
            u_slab=lambda j, ex=ex: upw[ex][11 + j],
            d_slab=lambda mo, ex=ex: (dnw[ex][mo], 0, 11 * 128),
            webk=webc)

def emit_sel8(X):
    P, A = X.P, X.A
    P.retag([("sel",), ("wet",), ("s8p",)], [("mask",)])
    t = A["amask"][:, 1024:2048]
    pi = A["asm"][:, 51:52]
    P.op("gpc", lambda e, t=t: e.iota(t, [[1, 8], [0, 128]], channel_multiplier=0,
                                      allow_small_or_imprecise_dtypes=True),
         writes=[("wet",)])
    P.op("gpc", lambda e, pi=pi: e.iota(pi, [[0, 1]], channel_multiplier=1,
                                        allow_small_or_imprecise_dtypes=True),
         writes=[("s8p",)])
    P.op("dve", lambda e, t=t, pi=pi: e.tensor_single_scalar(
        A["amask"][:, 0:1024], t, pi, ALU.is_equal),
        reads=[("wet",), ("s8p",)], writes=[("sel",)])


def emit_glu_group(X, n_j, g_slab, u_slab, d_slab, webk):
    P, A = X.P, X.A
    aw, at, aq = A["aw"], A["at"], A["aq"]
    for j in range(n_j):
        bi = j % 2
        gv, kg = X.wreg(bi), ("w", bi)
        uv, ku = X.wreg(2 + bi), ("w", 2 + bi)
        P.dma(f"w{bi}", "gpsimd", lambda e, gv=gv, j=j: e.dma_start(gv, g_slab(j)), writes=[kg])
        P.dma(f"w{2+bi}", "gpsimd", lambda e, uv=uv, j=j: e.dma_start(uv, u_slab(j)), writes=[ku])
        for c in range(C2):
            bg, bgk = X.next_bank()
            bu, buk = X.next_bank()
            mm_accum(P, bg[:, :], bgk,
                     [(gv[:, kt * 128:(kt + 1) * 128], X.xnc(kt, c), [kg, ("xn", kt, c)])
                      for kt in range(KT)])
            mm_accum(P, bu[:, :], buk,
                     [(uv[:, kt * 128:(kt + 1) * 128], X.xnc(kt, c), [ku, ("xn", kt, c)])
                      for kt in range(KT)])
            pp = (2 * j + c) % 2
            tg = at[:, pp * 1024: pp * 1024 + 512]
            tu = at[:, pp * 1024 + 512: pp * 1024 + 1024]
            P.op("act", lambda e, bg=bg, tg=tg: e.activation(
                tg, bg[:, :], AF.Silu, bias=0.0, scale=1.0),
                reads=[bgk], writes=atk(pp * 1024, pp * 1024 + 512))
            if webk is None:
                P.op("dve", lambda e, bu=bu, tu=tu: e.tensor_single_scalar(
                    tu, bu[:, :], 0.5, ALU.mult),
                    reads=[buk], writes=atk(pp * 1024 + 512, pp * 1024 + 1024))
            else:
                P.op("dve", lambda e, bu=bu, tu=tu, c=c: e.tensor_mul(
                    tu, bu[:, :], webk[:, c * 512:(c + 1) * 512]),
                    reads=[buk, ("cs",)], writes=atk(pp * 1024 + 512, pp * 1024 + 1024))
            P.op("dve", lambda e, tg=tg, tu=tu, j=j, c=c: e.tensor_mul(
                aq[:, j * T + c * 512: j * T + (c + 1) * 512], tg, tu),
                reads=atk(pp * 1024, pp * 1024 + 1024), writes=[("fu", j)])
    DN = [(2, "w4"), (3, "w5"), (None, "w2")]
    for mo in range(KT):
        dsl, koff, klen = d_slab(mo)
        reg, chan = DN[mo % 3]
        if reg is None:
            dv = A["awf"][:, :].bitcast(BF16)[:, 0:klen]
            kd = ("wf",)
        else:
            dv = X.wreg(reg)[:, 0:klen]
            kd = ("w", reg)
        P.dma(chan, "gpsimd",
              lambda e, dv=dv, dsl=dsl, koff=koff, klen=klen: e.dma_start(
                  dv, dsl[:, koff:koff + klen]), writes=[kd])
        for c in range(C2):
            bd, bdk = X.next_bank()
            mm_accum(P, bd[:, :], bdk,
                     [(dv[:, jj * 128:(jj + 1) * 128],
                       aq[:, jj * T + c * 512: jj * T + (c + 1) * 512],
                       [kd, ("fu", jj)]) for jj in range(n_j)])
            P.op("dve", lambda e, bd=bd, mo=mo, c=c: e.tensor_add(
                X.xc(mo, c), X.xc(mo, c), bd[:, :]),
                reads=[bdk, ("x", c)], writes=[("x", c)])


def kernel(**inputs):
    p = prep_inputs(inputs)
    xs = p["x_shards"]
    for grp in GROUPS:
        nc, in_names = build_program(grp)
        in_maps = []
        for c in range(8):
            m = {}
            for n in in_names:
                if n == "x_in":
                    m[n] = xs[c]
                elif n == "pos_off":
                    m[n] = p["pos_off"][c]
                else:
                    m[n] = p[n]
            in_maps.append(m)
        res = run_bass_kernel_spmd(nc, in_maps, core_ids=list(range(8)))
        xs = [np.asarray(res.results[c]["x_out"]) for c in range(8)]
    out = np.empty((4, 4, 512, 2048), np.float32)
    for c in range(8):
        xt = xs[c].T
        chs = CHUNKS[c % 2]
        out[c // 2, chs[0]] = xt[0:512]
        out[c // 2, chs[1]] = xt[512:1024]
    return np.ascontiguousarray(out.reshape(4, 2048, 2048))
